# version 01 (final)
# baseline (speedup 1.0000x reference)
import numpy as np
from contextlib import ExitStack

import concourse.bass as bass
import concourse.mybir as mybir
from concourse.bass_utils import run_bass_kernel_spmd
from concourse.alu_op_type import AluOpType

F32 = mybir.dt.float32
BF16 = mybir.dt.bfloat16
AF = mybir.ActivationFunctionType
AX = mybir.AxisListType

DBG = {}

NCORES = 8
TOK = 2048
D = 1024
NCH = D // 128
TT = 512
NTT = TOK // TT



BLK = 512


def core_token_idx(core):
    b, ph = core // 2, core % 2
    idx = []
    for l in range(4):
        blk = ph + 2 * l
        start = b * 4096 + blk * BLK
        idx.append(np.arange(start, start + BLK))
    return np.concatenate(idx)


def to_fm(x_core):
    xr = np.ascontiguousarray(x_core.T).reshape(NCH, 128, TOK)
    return np.ascontiguousarray(xr.transpose(1, 0, 2).reshape(128, NCH * TOK))


def from_fm(img):
    xr = img.reshape(128, NCH, TOK).transpose(1, 0, 2)
    return np.ascontiguousarray(xr.reshape(D, TOK).T)


def pack_panels(Wdindout, pw=128):
    din, dout = Wdindout.shape
    nk = din // 128
    no = dout // pw
    W4 = Wdindout.reshape(nk, 128, no, pw)
    return np.ascontiguousarray(W4.transpose(2, 1, 0, 3).reshape(no, 128, nk * pw))



class EngProxy:
    def __init__(self, blk, name):
        self._blk = blk
        self._name = name

    def _do(self, fn):
        out = []
        getattr(self._blk, self._name)(lambda e: out.append(fn(e)))
        return out[0]

    def __getattr__(self, meth):
        def wrap(*a, **k):
            return self._do(lambda e: getattr(e, meth)(*a, **k))
        return wrap


class Prog:
    def __init__(self, name):
        self.nc = bass.Bass("TRN2", target_bir_lowering=False)
        self.es = ExitStack()
        self.name = name
        self.sems = {}
        self.counts = {}
        self.inputs = {}

    def open_block(self):
        self.blk = self.es.enter_context(self.nc.Block())
        self.sy = EngProxy(self.blk, 'sync')
        self.t = EngProxy(self.blk, 'tensor')
        self.v = EngProxy(self.blk, 'vector')
        self.s = EngProxy(self.blk, 'scalar')
        self.g = EngProxy(self.blk, 'gpsimd')
        for n in ('L', 'G', 'T', 'V', 'A', 'C', 'V2'):
            self.sem(n)

    def sem(self, name):
        if name not in self.sems:
            self.sems[name] = self.es.enter_context(self.nc.semaphore(name))
            self.counts[name] = 0
        return self.sems[name]

    def dump(self, name, src_ap, shape, dtype=F32):
        if 'on' in DBG and ('only' not in DBG or name in DBG['only']):
            out = self.dram_out('dbg_' + name, shape, dtype)
            self.barrier(self.g)
            self.dma(self.g, out, src_ap, 'G')
            if 'nofence' not in DBG:
                for e in (self.t, self.v, self.s, self.sy):
                    self.wait(e, 'G')

    def dma(self, eng, out, in_, sem):
        i = eng.dma_start(out=out, in_=in_)
        return self.inc(i, sem, 16)

    def inc(self, instr, name, k=1):
        instr.then_inc(self.sem(name), k)
        self.counts[name] += k
        return self.counts[name]

    def wait(self, eng, name, val=None):
        self.sem(name)
        v = self.counts[name] if val is None else val
        if v > 0:
            eng.wait_ge(self.sems[name], v)

    def barrier(self, eng, names=('L', 'G', 'T', 'V', 'A')):
        for n in names:
            self.wait(eng, n)

    def fence(self):
        for e in (self.t, self.v, self.s, self.g, self.sy):
            self.barrier(e)

    def sbuf(self, name, shape, dtype=F32):
        return self.es.enter_context(self.nc.sbuf_tensor(name, shape, dtype))

    def psum(self, name, shape, dtype=F32):
        return self.es.enter_context(self.nc.psum_tensor(name, shape, dtype))

    def dram_in(self, name, arr):
        a0 = arr[0] if isinstance(arr, list) else arr
        t = self.nc.dram_tensor(name, list(a0.shape), mybir.dt.from_np(a0.dtype),
                                kind="ExternalInput")
        self.inputs[name] = arr
        return t.ap()

    def dram_out(self, name, shape, dtype=F32):
        return self.nc.dram_tensor(name, list(shape), dtype, kind="ExternalOutput").ap()

    def dram_tmp(self, name, shape, dtype=F32):
        return self.nc.dram_tensor(name, list(shape), dtype).ap()

    def in_maps(self):
        maps = []
        for c in range(NCORES):
            m = {}
            for k, v in self.inputs.items():
                m[k] = v[c] if isinstance(v, list) else v
            maps.append(m)
        return maps



def alloc_common(P):
    sc = {}
    sc['ones128'] = P.sbuf('ones128', [128, 1], F32)
    sc['onesrow'] = P.sbuf('onesrow', [1, 128], F32)
    sc['sq'] = P.sbuf('sq', [128, 2 * TT], F32)
    pool = P.psum('pspool', [128, 8 * TT], F32)
    sc['pspool'] = pool
    sc['ps_mm'] = pool[:, 0:TT]
    sc['ps_mm2'] = pool[:, TT:2 * TT]
    sc['ps_bc'] = pool[:, 2 * TT:4 * TT]
    sc['ps_stat'] = pool[0:1, 4 * TT:5 * TT]
    sc['ps_stat2'] = pool[0:1, 5 * TT:6 * TT]
    sc['ps_small'] = pool[:, 6 * TT:6 * TT + 80]
    sc['iota8'] = P.sbuf('iota8', [128, 8], F32)
    sc['iota8b'] = P.sbuf('iota8b', [128, 8], F32)
    sc['ustrict'] = P.sbuf('ustrict', [128, 128], F32)
    sc['ident'] = P.sbuf('ident', [128, 128], F32)
    return sc


def init_common(P, sc, consts):
    i1 = P.v.memset(sc['ones128'][:], 1.0)
    P.inc(i1, 'V')
    i2 = P.v.memset(sc['onesrow'][:], 1.0)
    P.inc(i2, 'V')
    for key in ('iota8', 'iota8b', 'ustrict', 'ident'):
        P.dma(P.sy, sc[key][:], consts[key], 'L')


def sumsq_over_chunks(P, sc, tiles, width, dst_psum):
    t, s = P.t, P.s
    sq = sc['sq']
    n = len(tiles)
    slot_t = sc.setdefault('sq_slot_t', [0, 0])
    for i, tile in enumerate(tiles):
        sl = i % 2
        if slot_t[sl]:
            s.wait_ge(P.sems['T'], slot_t[sl])
        i1 = s.activation(sq[:, sl * TT: sl * TT + width], tile, AF.Square)
        P.inc(i1, 'A')
        t.wait_ge(P.sems['A'], P.counts['A'])
        i2 = t.matmul(dst_psum[0:1, 0:width], sc['ones128'][:, 0:1],
                      sq[:, sl * TT: sl * TT + width],
                      start=(i == 0), stop=(i == n - 1))
        slot_t[sl] = P.inc(i2, 'T')


def colsum_over_chunks(P, sc, tiles, width, dst_psum):
    t = P.t
    n = len(tiles)
    for i, tile in enumerate(tiles):
        i2 = t.matmul(dst_psum[0:1, 0:width], sc['ones128'][:, 0:1], tile,
                      start=(i == 0), stop=(i == n - 1))
    P.inc(i2, 'T')


def bcast_row(P, sc, row_ap, dst_psum_slice):
    i = P.t.matmul(dst_psum_slice, sc['onesrow'][:, :], row_ap, start=True, stop=True)
    return P.inc(i, 'T')


def x_tile(xs, c, tt, width=TT):
    return xs[:, c * TOK + tt * TT: c * TOK + tt * TT + width]


def emit_postln(P, sc, xs):
    t, v, s = P.t, P.v, P.s
    for e in (t, v, s):
        P.barrier(e, ('T', 'V', 'A'))
    mean, var, tmp = sc['st_mean'], sc['st_var'], sc['st_tmp']
    a, b = var, tmp
    for tt in range(NTT):
        tiles = [x_tile(xs, c, tt) for c in range(NCH)]
        colsum_over_chunks(P, sc, tiles, TT, sc['ps_stat'])
        sumsq_over_chunks(P, sc, tiles, TT, sc['ps_stat2'])
        v.wait_ge(P.sems['T'], P.counts['T'])
        i1 = v.tensor_scalar_mul(mean[:, tt * TT:(tt + 1) * TT],
                                 sc['ps_stat'][0:1, 0:TT], 1.0 / D)
        P.inc(i1, 'V2')
        i2 = v.tensor_scalar_mul(var[:, tt * TT:(tt + 1) * TT],
                                 sc['ps_stat2'][0:1, 0:TT], 1.0 / D)
        P.inc(i2, 'V')
        P.inc2 = None
        t.wait_ge(P.sems['V'], P.counts['V'])
    s.wait_ge(P.sems['V'], P.counts['V'])
    s.wait_ge(P.sems['V2'], P.counts['V2'])
    i3 = s.activation(tmp[:, :], mean[:, :], AF.Square)
    P.inc(i3, 'A')
    v.wait_ge(P.sems['A'], P.counts['A'])
    i4 = v.tensor_sub(var[:, :], var[:, :], tmp[:, :])
    P.inc(i4, 'V')
    i4b = v.tensor_scalar_add(var[:, :], var[:, :], 1e-5)
    P.inc(i4b, 'V')
    P.dump('ln_var', var[:, :], [1, TOK])
    P.dump('ln_ex2', sc['st_tmp'][0:1, :], [1, TOK])
    s.wait_ge(P.sems['V'], P.counts['V'])
    i5 = s.activation(tmp[:, :], var[:, :], AF.Ln)
    P.inc(i5, 'A')
    v.wait_ge(P.sems['A'], P.counts['A'])
    i6 = s.activation(a[:, :], tmp[:, :], AF.Exp, scale=-0.5)
    P.inc(i6, 'A')
    s.wait_ge(P.sems['A'], P.counts['A'])
    v.wait_ge(P.sems['A'], P.counts['A'])
    i7 = v.tensor_mul(b[:, :], mean[:, :], a[:, :])
    P.inc(i7, 'V')
    i8 = v.tensor_scalar_mul(b[:, :], b[:, :], -1.0)
    P.inc(i8, 'V')
    P.dump('ln_mean', mean[:, :], [1, TOK])
    P.dump('ln_a', a[:, :], [1, TOK])
    P.dump('ln_b', b[:, :], [1, TOK])
    for tt in range(NTT):
        t.wait_ge(P.sems['V'], P.counts['V'])
        bcast_row(P, sc, a[:, tt * TT:(tt + 1) * TT], sc['ps_bc'][:, 0:TT])
        bcast_row(P, sc, b[:, tt * TT:(tt + 1) * TT], sc['ps_bc'][:, TT:2 * TT])
        v.wait_ge(P.sems['T'], P.counts['T'])
        for c in range(NCH):
            xt = x_tile(xs, c, tt)
            v.tensor_mul(xt, xt, sc['ps_bc'][:, 0:TT])
            ia = v.tensor_add(xt, xt, sc['ps_bc'][:, TT:2 * TT])
            P.inc(ia, 'V')



def prep_L0f(P, inputs):
    wup = np.asarray(inputs['L0_ffn_w_up'])
    wdn = np.asarray(inputs['L0_ffn_w_down'])
    return {
        'up_panels': P.dram_in('L0f_up', pack_panels(np.ascontiguousarray(wup.T))),
        'dn_panels': P.dram_in('L0f_dn', pack_panels(np.ascontiguousarray(wdn.T))),
        'tern_up': P.dram_tmp('L0f_tup', [32, 128, 1024], F32),
        'tern_dn': P.dram_tmp('L0f_tdn', [8, 128, 4096], F32),
    }


def alloc_statrows(P, sc):
    for r in ('st_mean', 'st_var', 'st_tmp'):
        sc[r] = P.sbuf(r, [1, TOK], F32)[0:1, :]
    sc['st_row'] = P.sbuf('st_row', [1, TT], F32)


def alloc_L0f(P, sc):
    alloc_statrows(P, sc)
    sc['wbuf'] = P.sbuf('wbuf', [128, 4096], F32)
    sc['acccol'] = P.sbuf('acccol', [128, 4], F32)
    sc['alpha'] = P.sbuf('alpha', [128, 2], F32)


def rms_recip_row(P, sc, s2_psum_row, inv_sqrt_d, dst_row):
    s, v = P.s, P.v
    s.wait_ge(P.sems['T'], P.counts['T'])
    i3 = s.activation(dst_row, s2_psum_row, AF.Sqrt)
    P.inc(i3, 'A')
    v.wait_ge(P.sems['A'], P.counts['A'])
    i4 = v.tensor_scalar(dst_row, dst_row, inv_sqrt_d, 1e-8,
                         AluOpType.mult, AluOpType.add)
    P.inc(i4, 'V')
    i5 = v.reciprocal(dst_row, dst_row)
    P.inc(i5, 'V')


def emit_L0f(P, sc, xs, wts):
    t, v, s, g, sy = P.t, P.v, P.s, P.g, P.sy
    ones128, onesrow = sc['ones128'], sc['onesrow']
    wbuf = sc['wbuf']
    acc, alpha = sc['acccol'], sc['alpha']
    ps, ps2, ps_bc = sc['ps_mm'], sc['ps_mm2'], sc['ps_bc']
    ps_s, ps_a = sc['ps_stat'], sc['ps_small']

    wtmp_cm = P.nc.sbuf_tensor('wtmp', [128, 4096], F32)
    wtmp = wtmp_cm.__enter__()
    for col, (panels, npan, klen, tern_out) in enumerate([
            (wts['up_panels'], 32, 1024, wts['tern_up']),
            (wts['dn_panels'], 8, 4096, wts['tern_dn'])]):
        nelem = float(4096 * 1024)
        for pnl in range(npan):
            sy.wait_ge(P.sems['A'], P.counts['A'])
            sy.wait_ge(P.sems['V'], P.counts['V'])
            P.dma(sy, wbuf[:, 0:klen], panels[pnl], 'L')
            s.wait_ge(P.sems['L'], P.counts['L'])
            s.wait_ge(P.sems['V'], P.counts['V'])
            ia = s.activation(wtmp[:, 0:klen], wbuf[:, 0:klen], AF.Abs)
            P.inc(ia, 'A')
            v.wait_ge(P.sems['A'], P.counts['A'])
            ir = v.tensor_reduce(acc[:, 1:2], wtmp[:, 0:klen], axis=AX.X, op=AluOpType.add)
            P.inc(ir, 'V')
            v.wait_ge(P.sems['V'], P.counts['V'])
            if pnl == 0:
                ic = v.tensor_copy(acc[:, 0:1], acc[:, 1:2])
            else:
                ic = v.tensor_add(acc[:, 0:1], acc[:, 0:1], acc[:, 1:2])
            P.inc(ic, 'V')
        t.wait_ge(P.sems['V'], P.counts['V'])
        i1 = t.matmul(ps_a[0:1, 0:1], acc[:, 0:1], ones128[:, 0:1], start=True, stop=True)
        P.inc(i1, 'T')
        v.wait_ge(P.sems['T'], P.counts['T'])
        iv = v.tensor_scalar_mul(acc[0:1, 2:3], ps_a[0:1, 0:1], 0.5 / nelem)
        P.inc(iv, 'V')
        t.wait_ge(P.sems['V'], P.counts['V'])
        ib = t.matmul(ps_a[:, 1:2], onesrow[:, :], acc[0:1, 2:3], start=True, stop=True)
        P.inc(ib, 'T')
        v.wait_ge(P.sems['T'], P.counts['T'])
        iv2 = v.tensor_copy(alpha[:, col:col + 1], ps_a[:, 1:2])
        P.inc(iv2, 'V')
        for pnl in range(npan):
            sy.wait_ge(P.sems['A'], P.counts['A'])
            sy.wait_ge(P.sems['V'], P.counts['V'])
            P.dma(sy, wbuf[:, 0:klen], panels[pnl], 'L')
            s.wait_ge(P.sems['L'], P.counts['L'])
            s.wait_ge(P.sems['G'], P.counts['G'])
            i1 = s.activation(wtmp[:, 0:klen], wbuf[:, 0:klen], AF.Abs)
            P.inc(i1, 'A')
            v.wait_ge(P.sems['A'], P.counts['A'])
            i2 = v.tensor_scalar(wtmp[:, 0:klen], wtmp[:, 0:klen],
                                 alpha[:, col:col + 1], None, AluOpType.is_gt)
            P.inc(i2, 'V')
            s.wait_ge(P.sems['V'], P.counts['V'])
            i3 = s.activation(wbuf[:, 0:klen], wbuf[:, 0:klen], AF.Sign)
            P.inc(i3, 'A')
            v.wait_ge(P.sems['A'], P.counts['A'])
            i4 = v.tensor_mul(wtmp[:, 0:klen], wtmp[:, 0:klen], wbuf[:, 0:klen])
            P.inc(i4, 'V')
            g.wait_ge(P.sems['V'], P.counts['V'])
            P.dma(g, tern_out[pnl], wtmp[:, 0:klen], 'G')

    P.dump('alpha', alpha[:, 0:2], [128, 2])
    P.dump('ternup0', wts['tern_up'][0], [128, 1024])
    P.dump('terndn0', wts['tern_dn'][0], [128, 4096])
    wtmp_cm.__exit__(None, None, None)
    h = P.sbuf('h', [128, 32 * TT], F32)
    xn = P.sbuf('xn', [128, NCH * TT], F32)
    sy.wait_ge(P.sems['G'], P.counts['G'])

    for tt in range(NTT):
        tiles = [x_tile(xs, c, tt) for c in range(NCH)]
        sumsq_over_chunks(P, sc, tiles, TT, ps_s)
        rms_recip_row(P, sc, ps_s[0:1, 0:TT], 1.0 / 32.0, sc['st_row'][0:1, 0:TT])
        t.wait_ge(P.sems['V'], P.counts['V'])
        bcast_row(P, sc, sc['st_row'][0:1, 0:TT], ps_bc[:, 0:TT])
        v.wait_ge(P.sems['T'], P.counts['T'])
        for c in range(NCH):
            iv = v.tensor_mul(xn[:, c * TT:(c + 1) * TT], tiles[c], ps_bc[:, 0:TT])
        P.inc(iv, 'V')

        if tt == 0:
            P.dump('xn0', xn[:, :], [128, NCH * TT])
        for oc in range(32):
            sy.wait_ge(P.sems['T'], P.counts['T'])
            P.dma(sy, wbuf[:, 0:1024], wts['tern_up'][oc], 'L')
            t.wait_ge(P.sems['L'], P.counts['L'])
            t.wait_ge(P.sems['V'], P.counts['V'])
            t.wait_ge(P.sems['A'], P.counts['A'])
            for k in range(NCH):
                i1 = t.matmul(ps[:, 0:TT], wbuf[:, k * 128:(k + 1) * 128],
                              xn[:, k * TT:(k + 1) * TT],
                              start=(k == 0), stop=(k == NCH - 1))
            P.inc(i1, 'T')
            s.wait_ge(P.sems['T'], P.counts['T'])
            ia = s.activation(h[:, oc * TT:(oc + 1) * TT], ps[:, 0:TT], AF.Gelu)
            P.inc(ia, 'A')

        if tt == 0:
            P.dump('h0', h[:, :], [128, 32 * TT])
        htiles = [h[:, c * TT:(c + 1) * TT] for c in range(32)]
        s.wait_ge(P.sems['A'], P.counts['A'])
        sumsq_over_chunks(P, sc, htiles, TT, ps_s)
        rms_recip_row(P, sc, ps_s[0:1, 0:TT], 1.0 / 64.0, sc['st_row'][0:1, 0:TT])
        t.wait_ge(P.sems['V'], P.counts['V'])
        bcast_row(P, sc, sc['st_row'][0:1, 0:TT], ps_bc[:, 0:TT])
        v.wait_ge(P.sems['T'], P.counts['T'])
        for c in range(32):
            iv = v.tensor_mul(htiles[c], htiles[c], ps_bc[:, 0:TT])
        P.inc(iv, 'V')

        if tt == 0:
            P.dump('hn0', h[:, :], [128, 32 * TT])
        for oc in range(NCH):
            sy.wait_ge(P.sems['T'], P.counts['T'])
            P.dma(sy, wbuf[:, 0:4096], wts['tern_dn'][oc], 'L')
            t.wait_ge(P.sems['L'], P.counts['L'])
            t.wait_ge(P.sems['V'], P.counts['V'])
            for k in range(32):
                i1 = t.matmul(ps2[:, 0:TT], wbuf[:, k * 128:(k + 1) * 128],
                              h[:, k * TT:(k + 1) * TT],
                              start=(k == 0), stop=(k == 31))
            P.inc(i1, 'T')
            v.wait_ge(P.sems['T'], P.counts['T'])
            xt = x_tile(xs, oc, tt)
            iv = v.tensor_add(xt, xt, ps2[:, 0:TT])
            P.inc(iv, 'V')

    P.dump('preln', xs[:, :], [128, NCH * TOK])
    emit_postln(P, sc, xs)




def prep_L2f(P, inputs):
    Win = np.asarray(inputs['L2_ffn_Win'])
    Wout = np.asarray(inputs['L2_ffn_Wout'])
    return {
        'win_panel': P.dram_in('L2f_win', pack_panels(Win, pw=64)[0]),
        'wout': P.dram_in('L2f_wout', np.ascontiguousarray(Wout)),
        'bin': P.dram_in('L2f_bin', np.ascontiguousarray(
            inputs['L2_ffn_bin_'].reshape(64, 1).astype(np.float32))),
        'bout': P.dram_in('L2f_bout', np.ascontiguousarray(
            inputs['L2_ffn_bout'].reshape(128, 8, order='F').astype(np.float32))),
        'w1': P.dram_in('L2f_w1', np.ascontiguousarray(inputs['L2_ffn_W1'].astype(np.float32))),
        'b1': P.dram_in('L2f_b1', np.ascontiguousarray(inputs['L2_ffn_b1'].astype(np.float32))),
        'w2': P.dram_in('L2f_w2', np.ascontiguousarray(inputs['L2_ffn_W2'].astype(np.float32))),
        'b2': P.dram_in('L2f_b2', np.ascontiguousarray(
            inputs['L2_ffn_b2'].reshape(64, 1).astype(np.float32))),
    }


def alloc_L2f(P, sc):
    alloc_statrows(P, sc)
    sc['l2_win'] = P.sbuf('l2_win', [128, 512], F32)
    sc['l2_wout'] = P.sbuf('l2_wout', [64, 1024], F32)
    sc['l2_bin'] = P.sbuf('l2_bin', [64, 1], F32)
    sc['l2_bout'] = P.sbuf('l2_bout', [128, 8], F32)
    sc['l2_w1'] = P.sbuf('l2_w1', [64, 16], F32)
    sc['l2_b1'] = P.sbuf('l2_b1', [64, 16], F32)
    sc['l2_w2'] = P.sbuf('l2_w2', [64, 16], F32)
    sc['l2_b2'] = P.sbuf('l2_b2', [64, 1], F32)
    sc['l2_s'] = P.sbuf('l2_s', [64, TOK], F32)
    sc['l2_g'] = P.sbuf('l2_g', [64, TOK], F32)
    sc['l2_tmp'] = P.sbuf('l2_tmp', [64, TOK], F32)


def emit_L2f(P, sc, xs, wts):
    t, v, s, g, sy = P.t, P.v, P.s, P.g, P.sy
    ps = sc['ps_mm']
    sfm, gacc, tmp = sc['l2_s'], sc['l2_g'], sc['l2_tmp']

    for name in ('win_panel', 'wout', 'bin', 'bout', 'w1', 'b1', 'w2', 'b2'):
        key = {'win_panel': 'l2_win', 'wout': 'l2_wout', 'bin': 'l2_bin',
               'bout': 'l2_bout', 'w1': 'l2_w1', 'b1': 'l2_b1',
               'w2': 'l2_w2', 'b2': 'l2_b2'}[name]
        P.dma(sy, sc[key][:], wts[name], 'L')

    for tt in range(NTT):
        t.wait_ge(P.sems['L'], P.counts['L'])
        t.wait_ge(P.sems['V'], P.counts['V'])
        for k in range(NCH):
            i1 = t.matmul(ps[0:64, 0:TT], sc['l2_win'][:, k * 64:(k + 1) * 64],
                          x_tile(xs, k, tt), start=(k == 0), stop=(k == NCH - 1))
        P.inc(i1, 'T')
        v.wait_ge(P.sems['T'], P.counts['T'])
        iv = v.tensor_scalar_add(sfm[:, tt * TT:(tt + 1) * TT], ps[0:64, 0:TT],
                                 sc['l2_bin'][:, 0:1])
        P.inc(iv, 'V')

    i0 = v.tensor_scalar(gacc[:, :], sfm[:, :], 0.0, sc['l2_b2'][:, 0:1],
                         AluOpType.mult, AluOpType.add)
    P.inc(i0, 'V')
    for j in range(16):
        v.tensor_scalar(tmp[:, :], sfm[:, :], sc['l2_w1'][:, j:j + 1],
                        sc['l2_b1'][:, j:j + 1], AluOpType.mult, AluOpType.add)
        v.tensor_scalar_max(tmp[:, :], tmp[:, :], 0.0)
        ij = v.scalar_tensor_tensor(gacc[:, :], tmp[:, :], sc['l2_w2'][:, j:j + 1],
                                    gacc[:, :], AluOpType.mult, AluOpType.add)
        P.inc(ij, 'V')
        v.wait_ge(P.sems['V'], P.counts['V'])

    P.dump('l2s', sfm[:, :], [64, TOK])
    P.dump('l2g', gacc[:, :], [64, TOK])
    for tt in range(NTT):
        for oc in range(NCH):
            t.wait_ge(P.sems['V'], P.counts['V'])
            i1 = t.matmul(ps[:, 0:TT], sc['l2_wout'][:, oc * 128:(oc + 1) * 128],
                          gacc[:, tt * TT:(tt + 1) * TT], start=True, stop=True)
            P.inc(i1, 'T')
            v.wait_ge(P.sems['T'], P.counts['T'])
            xt = x_tile(xs, oc, tt)
            iv = v.scalar_tensor_tensor(xt, ps[:, 0:TT], sc['l2_bout'][:, oc:oc + 1],
                                        xt, AluOpType.add, AluOpType.add)
            P.inc(iv, 'V')

    P.dump('l2preln', xs[:, :], [128, NCH * TOK])
    emit_postln(P, sc, xs)



CAPACITY = 5120
BIGC = 100.0
F32EPS = float(np.finfo(np.float32).eps)


def l1f_core_sel():
    sels = []
    for c in range(NCORES):
        b, ph = c // 2, c % 2
        s = np.zeros((64, 9), np.float32)
        for l in range(4):
            g_mine = b * 8 + (ph + 2 * l)
            for cp in range(NCORES):
                bp, php = cp // 2, cp % 2
                for lp in range(4):
                    gp = bp * 8 + (php + 2 * lp)
                    if gp < g_mine:
                        s[cp * 8 + lp * 2 + 0, 2 * l] = 1.0
                        s[cp * 8 + lp * 2 + 1, 2 * l + 1] = 1.0
        for cp in range(NCORES):
            for lp in range(4):
                s[cp * 8 + lp * 2 + 0, 8] = 1.0
        sels.append(s)
    return sels


def prep_L1f(P, inputs):
    w1 = np.asarray(inputs['L1_ffn_w1'])
    w2 = np.asarray(inputs['L1_ffn_w2'])
    sw1 = np.asarray(inputs['L1_ffn_sw1'])
    sw2 = np.asarray(inputs['L1_ffn_sw2'])
    up_pan = np.stack([pack_panels(w1[e]) for e in range(8)] + [pack_panels(sw1)])
    dn_pan = np.stack([pack_panels(w2[e]) for e in range(8)] + [pack_panels(sw2)])
    b1 = np.asarray(inputs['L1_ffn_b1'])
    sb1 = np.asarray(inputs['L1_ffn_sb1']).reshape(1, 4096)
    b1all = np.concatenate([b1, sb1], 0)
    b1c = np.ascontiguousarray(b1all.reshape(9, 32, 128).transpose(2, 0, 1).reshape(128, 288))
    b2all = np.concatenate([np.asarray(inputs['L1_ffn_b2']),
                            np.asarray(inputs['L1_ffn_sb2']).reshape(1, 1024)], 0)
    sel9 = np.zeros((9, 9 * 128), np.float32)
    for e in range(9):
        sel9[e, e * 128:(e + 1) * 128] = 1.0
    return {
        'router': P.dram_in('L1f_rw', pack_panels(
            np.asarray(inputs['L1_ffn_router_w']), pw=8)[0]),
        'up_pan': P.dram_in('L1f_up', up_pan),
        'dn_pan': P.dram_in('L1f_dn', dn_pan),
        'b1': P.dram_in('L1f_b1', b1c),
        'b2': P.dram_in('L1f_b2', np.ascontiguousarray(b2all.astype(np.float32))),
        'sel9': P.dram_in('L1f_sel9', sel9),
        'sel16': P.dram_in('L1f_sel16', l1f_core_sel()),
        'ag_in': P.dram_tmp('L1f_agin', [8, 8], F32),
        'ag_out': P.dram_tmp('L1f_agout', [64, 8], F32),
    }


def alloc_L1f(P, sc):
    sc['l1_rw'] = P.sbuf('l1_rw', [128, 64], F32)
    sc['l1_gates'] = P.sbuf('l1_gates', [128, 16 * 8], F32)
    sc['l1_m1'] = P.sbuf('l1_m1', [128, 16 * 8], F32)
    sc['l1_m2'] = P.sbuf('l1_m2', [128, 16 * 8], F32)
    sc['l1_gfm'] = P.sbuf('l1_gfm', [9, TOK], F32)
    sc['l1_rows'] = P.sbuf('l1_rows', [1, 80], F32)
    sc['l1_orow'] = P.sbuf('l1_orow', [1, 80], F32)
    sc['l1_cols'] = P.sbuf('l1_cols', [128, 16], F32)
    sc['l1_ag'] = P.sbuf('l1_ag', [64, 8], F32)
    sc['l1_sel16'] = P.sbuf('l1_sel16', [64, 9], F32)
    sc['l1_sel9'] = P.sbuf('l1_sel9', [9, 9 * 128], F32)
    sc['l1_b1'] = P.sbuf('l1_b1', [128, 288], F32)
    sc['l1_b2'] = P.sbuf('l1_b2', [9, 1024], F32)
    sc['l1_y'] = P.sbuf('l1_y', [128, NCH * TOK], F32)
    sc['l1_h'] = P.sbuf('l1_h', [128, 16 * TT], F32)
    sc['l1_wa'] = P.sbuf('l1_wa', [128, 2048], F32)
    sc['l1_wb'] = P.sbuf('l1_wb', [128, 2048], F32)
    sc['l1_gbc'] = P.sbuf('l1_gbc', [128, TT], F32)
    y = sc['l1_y']
    sc['st_mean'] = y[0:1, 0:TOK]
    sc['st_var'] = y[0:1, TOK:2 * TOK]
    sc['st_tmp'] = y[0:1, 2 * TOK:3 * TOK]


def emit_L1f(P, sc, xs, wts):
    t, v, s, g, sy = P.t, P.v, P.s, P.g, P.sy
    gates, m1t, m2t = sc['l1_gates'], sc['l1_m1'], sc['l1_m2']
    gfm, rows, cols = sc['l1_gfm'], sc['l1_rows'], sc['l1_cols']
    ps, ps2, ps_bc = sc['ps_mm'], sc['ps_mm2'], sc['ps_bc']
    ps_sm = sc['ps_small']
    h, yac, gbc = sc['l1_h'], sc['l1_y'], sc['l1_gbc']
    wb = [sc['l1_wa'], sc['l1_wb']]
    wb_gate = [0, 0]
    iota8 = sc['iota8']; iota8b = sc['iota8b']
    ustrict = sc['ustrict']; ident = sc['ident']

    for key, ap in [('l1_rw', wts['router']), ('l1_b1', wts['b1']),
                    ('l1_b2', wts['b2']), ('l1_sel16', wts['sel16']),
                    ('l1_sel9', wts['sel9'])]:
        P.dma(sy, sc[key][:], ap, 'L')

    for e in (t, v, s):
        P.barrier(e, ('T', 'V', 'A', 'L'))
    iz = v.memset(rows[0:1, 0:80], 0.0)
    P.inc(iz, 'V')
    for tile in range(16):
        t.wait_ge(P.sems['V'], P.counts['V'])
        for k in range(NCH):
            xt = xs[:, k * TOK + tile * 128: k * TOK + (tile + 1) * 128]
            i1 = t.matmul(ps_sm[:, 0:8], xt, sc['l1_rw'][:, k * 8:(k + 1) * 8],
                          start=(k == 0), stop=(k == NCH - 1))
        P.inc(i1, 'T')
        gt = gates[:, tile * 8:(tile + 1) * 8]
        mk1 = m1t[:, tile * 8:(tile + 1) * 8]
        mk2 = m2t[:, tile * 8:(tile + 1) * 8]
        v.wait_ge(P.sems['T'], P.counts['T'])
        ir = v.reduce_max(cols[:, 0:1], ps_sm[:, 0:8], axis=AX.X)
        P.inc(ir, 'V')
        v.wait_ge(P.sems['V'], P.counts['V'])
        ineg = v.tensor_scalar_mul(cols[:, 1:2], cols[:, 0:1], -1.0)
        P.inc(ineg, 'V')
        s.wait_ge(P.sems['V'], P.counts['V'])
        ie = s.activation(gt, ps_sm[:, 0:8], AF.Exp, bias=cols[:, 1:2])
        P.inc(ie, 'A')
        v.wait_ge(P.sems['A'], P.counts['A'])
        is_ = v.reduce_sum(cols[:, 2:3], gt, axis=AX.X)
        P.inc(is_, 'V')
        v.wait_ge(P.sems['V'], P.counts['V'])
        irc = v.reciprocal(cols[:, 3:4], cols[:, 2:3])
        P.inc(irc, 'V')
        v.wait_ge(P.sems['V'], P.counts['V'])
        ig = v.tensor_scalar_mul(gt, gt, cols[:, 3:4])
        P.inc(ig, 'V')
        ir1 = v.reduce_max(cols[:, 4:5], gt, axis=AX.X)
        P.inc(ir1, 'V')
        v.wait_ge(P.sems['V'], P.counts['V'])
        ieq = v.tensor_scalar(cols[:, 8:16], gt, cols[:, 4:5], None, AluOpType.is_ge)
        ist = v.scalar_tensor_tensor(cols[:, 8:16], cols[:, 8:16], -BIGC,
                                     iota8b[:, 0:8], AluOpType.mult, AluOpType.add)
        imn = v.tensor_reduce(cols[:, 5:6], cols[:, 8:16], axis=AX.X, op=AluOpType.min)
        P.inc(imn, 'V')
        v.wait_ge(P.sems['V'], P.counts['V'])
        im1 = v.tensor_scalar(mk1, iota8[:, 0:8], cols[:, 5:6], None, AluOpType.is_equal)
        i1m = v.tensor_scalar(cols[:, 8:16], mk1, -1.0, 1.0,
                              AluOpType.mult, AluOpType.add)
        ig2 = v.tensor_mul(cols[:, 8:16], gt, cols[:, 8:16])
        ir2 = v.reduce_max(cols[:, 6:7], cols[:, 8:16], axis=AX.X)
        P.inc(ir2, 'V')
        v.wait_ge(P.sems['V'], P.counts['V'])
        ieq2 = v.tensor_scalar(cols[:, 8:16], cols[:, 8:16], cols[:, 6:7], None,
                               AluOpType.is_ge)
        ist2 = v.scalar_tensor_tensor(cols[:, 8:16], cols[:, 8:16], -BIGC,
                                      iota8b[:, 0:8], AluOpType.mult, AluOpType.add)
        imn2 = v.tensor_reduce(cols[:, 7:8], cols[:, 8:16], axis=AX.X, op=AluOpType.min)
        P.inc(imn2, 'V')
        v.wait_ge(P.sems['V'], P.counts['V'])
        im2 = v.tensor_scalar(mk2, iota8[:, 0:8], cols[:, 7:8], None, AluOpType.is_equal)
        P.inc(im2, 'V')
        t.wait_ge(P.sems['V'], P.counts['V'])
        it1 = t.matmul(sc['ps_stat'][0:1, 0:8], sc['ones128'][:, 0:1], mk1,
                       start=True, stop=True)
        P.inc(it1, 'T')
        it2 = t.matmul(sc['ps_stat2'][0:1, 0:8], sc['ones128'][:, 0:1], mk2,
                       start=True, stop=True)
        P.inc(it2, 'T')
        v.wait_ge(P.sems['T'], P.counts['T'])
        lb = tile // 4
        ia1 = v.tensor_add(rows[0:1, 16 * lb:16 * lb + 8],
                           rows[0:1, 16 * lb:16 * lb + 8], sc['ps_stat'][0:1, 0:8])
        ia2 = v.tensor_add(rows[0:1, 16 * lb + 8:16 * lb + 16],
                           rows[0:1, 16 * lb + 8:16 * lb + 16], sc['ps_stat2'][0:1, 0:8])
        P.inc(ia2, 'V')
        t.wait_ge(P.sems['V'], P.counts['V'])

    P.barrier(g)
    for lb in range(4):
        P.dma(g, wts['ag_in'][2 * lb:2 * lb + 1, :], rows[0:1, 16 * lb:16 * lb + 8], 'G')
        P.dma(g, wts['ag_in'][2 * lb + 1:2 * lb + 2, :],
              rows[0:1, 16 * lb + 8:16 * lb + 16], 'G')
    g.wait_ge(P.sems['G'], P.counts['G'])
    icc = g.collective_compute(
        "AllGather", AluOpType.bypass, replica_groups=[list(range(NCORES))],
        ins=[wts['ag_in'].opt()], outs=[wts['ag_out'].opt()])
    P.inc(icc, 'C', 1)
    g.wait_ge(P.sems['C'], P.counts['C'])
    P.dma(g, sc['l1_ag'][:], wts['ag_out'], 'G')
    for e in (t, v, s, sy):
        P.wait(e, 'G')

    orow = sc['l1_orow']
    for j in range(9):
        ij = t.matmul(ps_sm[0:1, 8 * j:8 * j + 8], sc['l1_sel16'][:, j:j + 1],
                      sc['l1_ag'][:, 0:8], start=True, stop=True)
        P.inc(ij, 'T')
    v.wait_ge(P.sems['T'], P.counts['T'])
    imn = v.tensor_scalar_min(orow[0:1, 64:72], ps_sm[0:1, 64:72], float(CAPACITY))
    P.inc(imn, 'V')
    v.wait_ge(P.sems['V'], P.counts['V'])
    for lb in range(4):
        ic1 = v.tensor_copy(orow[0:1, 16 * lb:16 * lb + 8],
                            ps_sm[0:1, 16 * lb:16 * lb + 8])
        ic2 = v.tensor_add(orow[0:1, 16 * lb + 8:16 * lb + 16],
                           ps_sm[0:1, 16 * lb + 8:16 * lb + 16], orow[0:1, 64:72])
        P.inc(ic2, 'V')

    ishg = v.memset(gfm[0:9, :], 0.1)
    P.inc(ishg, 'V')

    for tile in range(16):
        gt = gates[:, tile * 8:(tile + 1) * 8]
        mk1 = m1t[:, tile * 8:(tile + 1) * 8]
        mk2 = m2t[:, tile * 8:(tile + 1) * 8]
        if tile % 4 == 0:
            lb = tile // 4
            v.wait_ge(P.sems['T'], P.counts['T'])
            ir1 = v.tensor_copy(rows[0:1, 64:72], orow[0:1, 16 * lb:16 * lb + 8])
            ir2 = v.tensor_copy(rows[0:1, 72:80], orow[0:1, 16 * lb + 8:16 * lb + 16])
            P.inc(ir2, 'V')
        t.wait_ge(P.sems['V'], P.counts['V'])
        ip1 = t.matmul(ps_sm[:, 0:8], ustrict[:, :], mk1, start=True, stop=False)
        ip1b = t.matmul(ps_sm[:, 0:8], sc['onesrow'][:, :], rows[0:1, 64:72],
                        start=False, stop=True)
        P.inc(ip1b, 'T')
        ip2 = t.matmul(ps_sm[:, 8:16], ustrict[:, :], mk2, start=True, stop=False)
        ip2b = t.matmul(ps_sm[:, 8:16], sc['onesrow'][:, :], rows[0:1, 72:80],
                        start=False, stop=True)
        P.inc(ip2b, 'T')
        it1 = t.matmul(sc['ps_stat'][0:1, 0:8], sc['ones128'][:, 0:1], mk1,
                       start=True, stop=True)
        P.inc(it1, 'T')
        it2 = t.matmul(sc['ps_stat2'][0:1, 0:8], sc['ones128'][:, 0:1], mk2,
                       start=True, stop=True)
        P.inc(it2, 'T')
        v.wait_ge(P.sems['T'], P.counts['T'])
        il1 = v.tensor_scalar(cols[:, 8:16], ps_sm[:, 0:8], float(CAPACITY), None,
                              AluOpType.is_lt)
        ik1 = v.tensor_mul(mk1, mk1, cols[:, 8:16])
        il2 = v.tensor_scalar(cols[:, 8:16], ps_sm[:, 8:16], float(CAPACITY), None,
                              AluOpType.is_lt)
        ik2 = v.tensor_mul(mk2, mk2, cols[:, 8:16])
        iu1 = v.tensor_add(rows[0:1, 16:24], rows[0:1, 16:24], sc['ps_stat'][0:1, 0:8])
        iu2 = v.tensor_add(rows[0:1, 24:32], rows[0:1, 24:32], sc['ps_stat2'][0:1, 0:8])
        P.inc(iu2, 'V')
        ig1m = v.tensor_mul(cols[:, 8:16], gt, mk1)
        ig1 = v.reduce_sum(cols[:, 0:1], cols[:, 8:16], axis=AX.X)
        P.inc(ig1, 'V')
        v.wait_ge(P.sems['V'], P.counts['V'])
        ig2m = v.tensor_mul(cols[:, 8:16], gt, mk2)
        ig2 = v.reduce_sum(cols[:, 1:2], cols[:, 8:16], axis=AX.X)
        P.inc(ig2, 'V')
        v.wait_ge(P.sems['V'], P.counts['V'])
        idn = v.tensor_add(cols[:, 2:3], cols[:, 0:1], cols[:, 1:2])
        idm = v.tensor_scalar_max(cols[:, 2:3], cols[:, 2:3], F32EPS)
        irc = v.reciprocal(cols[:, 3:4], cols[:, 2:3])
        P.inc(irc, 'V')
        v.wait_ge(P.sems['V'], P.counts['V'])
        in1 = v.tensor_scalar_mul(cols[:, 0:1], cols[:, 0:1], cols[:, 3:4])
        in2 = v.tensor_scalar_mul(cols[:, 1:2], cols[:, 1:2], cols[:, 3:4])
        P.inc(in2, 'V')
        v.wait_ge(P.sems['V'], P.counts['V'])
        ifg1 = v.tensor_scalar_mul(mk1, mk1, cols[:, 0:1])
        ifg2 = v.tensor_scalar_mul(mk2, mk2, cols[:, 1:2])
        ifg = v.tensor_add(gt, mk1, mk2)
        P.inc(ifg, 'V')
        t.wait_ge(P.sems['V'], P.counts['V'])
        itr = t.transpose(ps_bc[0:8, 0:128], gt, ident[:, :])
        P.inc(itr, 'T')
        v.wait_ge(P.sems['T'], P.counts['T'])
        iev = v.tensor_copy(gfm[0:8, tile * 128:(tile + 1) * 128], ps_bc[0:8, 0:128])
        P.inc(iev, 'V')
        t.wait_ge(P.sems['V'], P.counts['V'])

    for tt in range(NTT):
        for oc in range(NCH):
            t.wait_ge(P.sems['V'], P.counts['V'])
            ib = t.matmul(ps2[:, 0:TT], sc['l1_b2'][:, oc * 128:(oc + 1) * 128],
                          gfm[:, tt * TT:(tt + 1) * TT], start=True, stop=True)
            P.inc(ib, 'T')
            v.wait_ge(P.sems['T'], P.counts['T'])
            iy = v.tensor_copy(yac[:, oc * TOK + tt * TT: oc * TOK + (tt + 1) * TT],
                               ps2[:, 0:TT])
            P.inc(iy, 'V')

    for e9 in range(9):
        for tt in range(NTT):
            t.wait_ge(P.sems['V'], P.counts['V'])
            igb = t.matmul(ps_bc[:, 0:TT], sc['l1_sel9'][:, e9 * 128:(e9 + 1) * 128],
                           gfm[:, tt * TT:(tt + 1) * TT], start=True, stop=True)
            P.inc(igb, 'T')
            v.wait_ge(P.sems['T'], P.counts['T'])
            igc = v.tensor_copy(gbc[:, 0:TT], ps_bc[:, 0:TT])
            P.inc(igc, 'V')
            for half in range(2):
                for oc in range(16):
                    ocg = half * 16 + oc
                    buf = wb[oc % 2]
                    sy.wait_ge(P.sems['T'], wb_gate[oc % 2])
                    P.dma(sy, buf[:, 0:1024], wts['up_pan'][e9, ocg], 'L')
                    t.wait_ge(P.sems['L'], P.counts['L'])
                    t.wait_ge(P.sems['A'], P.counts['A'])
                    for k in range(NCH):
                        i1 = t.matmul(ps[:, 0:TT], buf[:, k * 128:(k + 1) * 128],
                                      xs[:, k * TOK + tt * TT:k * TOK + (tt + 1) * TT],
                                      start=(k == 0), stop=(k == NCH - 1))
                    wb_gate[oc % 2] = P.inc(i1, 'T')
                    s.wait_ge(P.sems['T'], P.counts['T'])
                    isl = s.activation(h[:, oc * TT:(oc + 1) * TT], ps[:, 0:TT],
                                       AF.Silu, bias=sc['l1_b1'][:, e9 * 32 + ocg:e9 * 32 + ocg + 1])
                    P.inc(isl, 'A')
                for oc in range(NCH):
                    buf = wb[oc % 2]
                    sy.wait_ge(P.sems['T'], wb_gate[oc % 2])
                    P.dma(sy, buf[:, 0:2048],
                          wts['dn_pan'][e9, oc][:, half * 2048:(half + 1) * 2048], 'L')
                    t.wait_ge(P.sems['L'], P.counts['L'])
                    t.wait_ge(P.sems['A'], P.counts['A'])
                    t.wait_ge(P.sems['V'], P.counts['V'])
                    for k in range(16):
                        i2 = t.matmul(ps2[:, 0:TT], buf[:, k * 128:(k + 1) * 128],
                                      h[:, k * TT:(k + 1) * TT],
                                      start=(k == 0), stop=(k == 15))
                    wb_gate[oc % 2] = P.inc(i2, 'T')
                    v.wait_ge(P.sems['T'], P.counts['T'])
                    ytile = yac[:, oc * TOK + tt * TT: oc * TOK + (tt + 1) * TT]
                    imul = v.tensor_mul(sc['sq'][:, 0:TT], ps2[:, 0:TT], gbc[:, 0:TT])
                    iacc = v.tensor_add(ytile, ytile, sc['sq'][:, 0:TT])
                    P.inc(iacc, 'V')

    for e in (v,):
        P.barrier(e, ('T', 'A'))
    for oc in range(NCH):
        for tt in range(NTT):
            xt = x_tile(xs, oc, tt)
            iv = v.tensor_add(xt, xt, yac[:, oc * TOK + tt * TT: oc * TOK + (tt + 1) * TT])
            P.inc(iv, 'V')
    emit_postln(P, sc, xs)



def prep_L3f(P, inputs):
    guw = np.asarray(inputs['L3_ffn_gate_up_w'])
    dnw = np.asarray(inputs['L3_ffn_down_w'])
    gu_pan = np.stack([pack_panels(np.ascontiguousarray(guw[e].T)) for e in range(8)])
    dn_pan = np.stack([pack_panels(np.ascontiguousarray(dnw[e].T)) for e in range(8)])
    sel8 = np.zeros((8, 8 * 128), np.float32)
    for e in range(8):
        sel8[e, e * 128:(e + 1) * 128] = 1.0
    gm = np.asarray(inputs['L3_ffn_ln_gamma']).reshape(8, 128).T.copy()
    bt = np.asarray(inputs['L3_ffn_ln_beta']).reshape(8, 128).T.copy()
    return {
        'router': P.dram_in('L3f_rw', pack_panels(
            np.ascontiguousarray(np.asarray(inputs['L3_ffn_router_w']).T), pw=8)[0]),
        'gu_pan': P.dram_in('L3f_gu', gu_pan),
        'dn_pan': P.dram_in('L3f_dn', dn_pan),
        'sel8': P.dram_in('L3f_sel8', sel8),
        'gamma': P.dram_in('L3f_g', np.ascontiguousarray(gm)),
        'beta': P.dram_in('L3f_b', np.ascontiguousarray(bt)),
    }


def alloc_L3f(P, sc):
    sc['l3_rw'] = P.sbuf('l3_rw', [128, 64], F32)
    sc['l3_gates'] = P.sbuf('l3_gates', [128, 16 * 8], F32)
    sc['l3_m1'] = P.sbuf('l3_m1', [128, 16 * 8], F32)
    sc['l3_gfm'] = P.sbuf('l3_gfm', [8, TOK], F32)
    sc['l3_cols'] = P.sbuf('l3_cols', [128, 16], F32)
    sc['l3_sel8'] = P.sbuf('l3_sel8', [8, 8 * 128], F32)
    sc['l3_gamma'] = P.sbuf('l3_gamma', [128, 8], F32)
    sc['l3_beta'] = P.sbuf('l3_beta', [128, 8], F32)
    sc['l3_xn'] = P.sbuf('l3_xn', [128, NCH * TOK], F32)
    sc['l3_h'] = P.sbuf('l3_h', [128, 8 * TT], F32)
    sc['l3_xe'] = P.sbuf('l3_xe', [128, NCH * TT], F32)
    sc['l3_wa'] = P.sbuf('l3_wa', [128, 2048], F32)
    sc['l3_wb'] = P.sbuf('l3_wb', [128, 2048], F32)
    sc['l3_gbc'] = P.sbuf('l3_gbc', [128, TT], F32)
    sc['st_mean'] = sc['l3_h'][0:1, 0:TOK]
    sc['st_var'] = sc['l3_h'][0:1, TOK:2 * TOK]
    sc['st_tmp'] = sc['l3_xe'][0:1, 0:TOK]


def emit_L3f(P, sc, xs, wts):
    t, v, s, g, sy = P.t, P.v, P.s, P.g, P.sy
    gates, mk1t = sc['l3_gates'], sc['l3_m1']
    gfm, cols = sc['l3_gfm'], sc['l3_cols']
    ps, ps2, ps_bc = sc['ps_mm'], sc['ps_mm2'], sc['ps_bc']
    ps_sm = sc['ps_small']
    h, xn, xe, gbc = sc['l3_h'], sc['l3_xn'], sc['l3_xe'], sc['l3_gbc']
    wb = [sc['l3_wa'], sc['l3_wb']]
    wb_gate = [0, 0]
    iota8 = sc['iota8']; iota8b = sc['iota8b']; ident = sc['ident']
    mean, var, tmp = sc['st_mean'], sc['st_var'], sc['st_tmp']
    a, b = var, tmp
    REPS = 1e-08
    CLAMP = 10000.0

    for key, ap in [('l3_rw', wts['router']), ('l3_sel8', wts['sel8']),
                    ('l3_gamma', wts['gamma']), ('l3_beta', wts['beta'])]:
        P.dma(sy, sc[key][:], ap, 'L')

    for e in (t, v, s):
        P.barrier(e, ('T', 'V', 'A', 'L'))
    for tt in range(NTT):
        tiles = [x_tile(xs, c, tt) for c in range(NCH)]
        colsum_over_chunks(P, sc, tiles, TT, sc['ps_stat'])
        sumsq_over_chunks(P, sc, tiles, TT, sc['ps_stat2'])
        v.wait_ge(P.sems['T'], P.counts['T'])
        i1 = v.tensor_scalar_mul(mean[:, tt * TT:(tt + 1) * TT],
                                 sc['ps_stat'][0:1, 0:TT], 1.0 / D)
        i2 = v.tensor_scalar_mul(var[:, tt * TT:(tt + 1) * TT],
                                 sc['ps_stat2'][0:1, 0:TT], 1.0 / D)
        P.inc(i2, 'V')
        t.wait_ge(P.sems['V'], P.counts['V'])
    s.wait_ge(P.sems['V'], P.counts['V'])
    i3 = s.activation(tmp[:, :], mean[:, :], AF.Square)
    P.inc(i3, 'A')
    v.wait_ge(P.sems['A'], P.counts['A'])
    i4 = v.tensor_sub(var[:, :], var[:, :], tmp[:, :])
    i4b = v.tensor_scalar_add(var[:, :], var[:, :], 1e-5)
    P.inc(i4b, 'V')
    s.wait_ge(P.sems['V'], P.counts['V'])
    i5 = s.activation(tmp[:, :], var[:, :], AF.Ln)
    P.inc(i5, 'A')
    v.wait_ge(P.sems['A'], P.counts['A'])
    i6 = s.activation(a[:, :], tmp[:, :], AF.Exp, scale=-0.5)
    P.inc(i6, 'A')
    v.wait_ge(P.sems['A'], P.counts['A'])
    i7 = v.tensor_mul(b[:, :], mean[:, :], a[:, :])
    i8 = v.tensor_scalar_mul(b[:, :], b[:, :], -1.0)
    P.inc(i8, 'V')
    for tt in range(NTT):
        t.wait_ge(P.sems['V'], P.counts['V'])
        bcast_row(P, sc, a[:, tt * TT:(tt + 1) * TT], sc['ps_bc'][:, 0:TT])
        bcast_row(P, sc, b[:, tt * TT:(tt + 1) * TT], sc['ps_bc'][:, TT:2 * TT])
        v.wait_ge(P.sems['T'], P.counts['T'])
        for c in range(NCH):
            xt = x_tile(xs, c, tt)
            xnt = xn[:, c * TOK + tt * TT: c * TOK + (tt + 1) * TT]
            v.tensor_mul(xnt, xt, sc['ps_bc'][:, 0:TT])
            v.tensor_add(xnt, xnt, sc['ps_bc'][:, TT:2 * TT])
            ia = v.tensor_scalar(xnt, xnt, sc['l3_gamma'][:, c:c + 1],
                                 sc['l3_beta'][:, c:c + 1],
                                 AluOpType.mult, AluOpType.add)
            P.inc(ia, 'V')
        t.wait_ge(P.sems['V'], P.counts['V'])

    for tile in range(16):
        t.wait_ge(P.sems['V'], P.counts['V'])
        for k in range(NCH):
            xnt = xn[:, k * TOK + tile * 128: k * TOK + (tile + 1) * 128]
            i1 = t.matmul(ps_sm[:, 0:8], xnt, sc['l3_rw'][:, k * 8:(k + 1) * 8],
                          start=(k == 0), stop=(k == NCH - 1))
        P.inc(i1, 'T')
        gt = gates[:, tile * 8:(tile + 1) * 8]
        mk1 = mk1t[:, tile * 8:(tile + 1) * 8]
        v.wait_ge(P.sems['T'], P.counts['T'])
        icl = v.tensor_scalar(gt, ps_sm[:, 0:8], CLAMP, -CLAMP,
                              AluOpType.min, AluOpType.max)
        P.inc(icl, 'V')
        ir = v.reduce_max(cols[:, 0:1], gt, axis=AX.X)
        P.inc(ir, 'V')
        v.wait_ge(P.sems['V'], P.counts['V'])
        ineg = v.tensor_scalar_mul(cols[:, 1:2], cols[:, 0:1], -1.0)
        P.inc(ineg, 'V')
        s.wait_ge(P.sems['V'], P.counts['V'])
        ie = s.activation(gt, gt, AF.Exp, bias=cols[:, 1:2])
        P.inc(ie, 'A')
        v.wait_ge(P.sems['A'], P.counts['A'])
        is_ = v.reduce_sum(cols[:, 2:3], gt, axis=AX.X)
        P.inc(is_, 'V')
        v.wait_ge(P.sems['V'], P.counts['V'])
        ise = v.tensor_scalar_add(cols[:, 2:3], cols[:, 2:3], REPS)
        P.inc(ise, 'V')
        irc = v.reciprocal(cols[:, 3:4], cols[:, 2:3])
        P.inc(irc, 'V')
        v.wait_ge(P.sems['V'], P.counts['V'])
        ig = v.tensor_scalar_mul(gt, gt, cols[:, 3:4])
        P.inc(ig, 'V')
        igc = v.tensor_scalar(gt, gt, REPS, 1.0, AluOpType.max, AluOpType.min)
        P.inc(igc, 'V')
        ir1 = v.reduce_max(cols[:, 4:5], gt, axis=AX.X)
        P.inc(ir1, 'V')
        v.wait_ge(P.sems['V'], P.counts['V'])
        ieq = v.tensor_scalar(cols[:, 8:16], gt, cols[:, 4:5], None, AluOpType.is_ge)
        ist = v.scalar_tensor_tensor(cols[:, 8:16], cols[:, 8:16], -BIGC,
                                     iota8b[:, 0:8], AluOpType.mult, AluOpType.add)
        imn = v.tensor_reduce(cols[:, 5:6], cols[:, 8:16], axis=AX.X, op=AluOpType.min)
        P.inc(imn, 'V')
        v.wait_ge(P.sems['V'], P.counts['V'])
        im1 = v.tensor_scalar(mk1, iota8[:, 0:8], cols[:, 5:6], None, AluOpType.is_equal)
        i1m = v.tensor_scalar(cols[:, 8:16], mk1, -1.0, 1.0,
                              AluOpType.mult, AluOpType.add)
        ig2 = v.tensor_mul(cols[:, 8:16], gt, cols[:, 8:16])
        ir2 = v.reduce_max(cols[:, 6:7], cols[:, 8:16], axis=AX.X)
        P.inc(ir2, 'V')
        v.wait_ge(P.sems['V'], P.counts['V'])
        ieq2 = v.tensor_scalar(cols[:, 8:16], cols[:, 8:16], cols[:, 6:7], None,
                               AluOpType.is_ge)
        ist2 = v.scalar_tensor_tensor(cols[:, 8:16], cols[:, 8:16], -BIGC,
                                      iota8b[:, 0:8], AluOpType.mult, AluOpType.add)
        imn2 = v.tensor_reduce(cols[:, 7:8], cols[:, 8:16], axis=AX.X, op=AluOpType.min)
        P.inc(imn2, 'V')
        v.wait_ge(P.sems['V'], P.counts['V'])
        im2 = v.tensor_scalar(cols[:, 8:16], iota8[:, 0:8], cols[:, 7:8], None,
                              AluOpType.is_equal)
        P.inc(im2, 'V')
        ifg1 = v.tensor_scalar_mul(mk1, mk1, cols[:, 4:5])
        ifg2 = v.tensor_scalar_mul(cols[:, 8:16], cols[:, 8:16], cols[:, 6:7])
        ifg = v.tensor_add(gt, mk1, cols[:, 8:16])
        P.inc(ifg, 'V')
        t.wait_ge(P.sems['V'], P.counts['V'])
        itr = t.transpose(ps_bc[0:8, 0:128], gt, ident[:, :])
        P.inc(itr, 'T')
        v.wait_ge(P.sems['T'], P.counts['T'])
        iev = v.tensor_copy(gfm[0:8, tile * 128:(tile + 1) * 128], ps_bc[0:8, 0:128])
        P.inc(iev, 'V')
        t.wait_ge(P.sems['V'], P.counts['V'])

    for oc in range(NCH):
        for tt in range(NTT):
            xt = x_tile(xs, oc, tt)
            iv = v.tensor_scalar_mul(xt, xt, 2.0)
            P.inc(iv, 'V')

    for e8 in range(8):
        for tt in range(NTT):
            t.wait_ge(P.sems['V'], P.counts['V'])
            igb = t.matmul(ps_bc[:, 0:TT], sc['l3_sel8'][:, e8 * 128:(e8 + 1) * 128],
                           gfm[:, tt * TT:(tt + 1) * TT], start=True, stop=True)
            P.inc(igb, 'T')
            v.wait_ge(P.sems['T'], P.counts['T'])
            igc2 = v.tensor_copy(gbc[:, 0:TT], ps_bc[:, 0:TT])
            P.inc(igc2, 'V')
            v.wait_ge(P.sems['V'], P.counts['V'])
            v.wait_ge(P.sems['T'], P.counts['T'])
            for c in range(NCH):
                ix = v.tensor_mul(xe[:, c * TT:(c + 1) * TT],
                                  xn[:, c * TOK + tt * TT: c * TOK + (tt + 1) * TT],
                                  gbc[:, 0:TT])
                P.inc(ix, 'V')
            for half in range(2):
                for oc in range(8):
                    ocg = half * 8 + oc
                    buf = wb[oc % 2]
                    sy.wait_ge(P.sems['T'], wb_gate[oc % 2])
                    P.dma(sy, buf[:, 0:1024], wts['gu_pan'][e8, ocg], 'L')
                    t.wait_ge(P.sems['L'], P.counts['L'])
                    t.wait_ge(P.sems['A'], P.counts['A'])
                    t.wait_ge(P.sems['V'], P.counts['V'])
                    for k in range(NCH):
                        i1 = t.matmul(ps[:, 0:TT], buf[:, k * 128:(k + 1) * 128],
                                      xe[:, k * TT:(k + 1) * TT],
                                      start=(k == 0), stop=(k == NCH - 1))
                    wb_gate[oc % 2] = P.inc(i1, 'T')
                    s.wait_ge(P.sems['T'], P.counts['T'])
                    isl = s.activation(h[:, oc * TT:(oc + 1) * TT], ps[:, 0:TT], AF.Silu)
                    P.inc(isl, 'A')
                for oc in range(8):
                    ocg = half * 8 + oc
                    buf = wb[oc % 2]
                    sy.wait_ge(P.sems['T'], wb_gate[oc % 2])
                    P.dma(sy, buf[:, 0:1024], wts['gu_pan'][e8, 16 + ocg], 'L')
                    t.wait_ge(P.sems['L'], P.counts['L'])
                    t.wait_ge(P.sems['A'], P.counts['A'])
                    t.wait_ge(P.sems['V'], P.counts['V'])
                    for k in range(NCH):
                        i1 = t.matmul(ps[:, 0:TT], buf[:, k * 128:(k + 1) * 128],
                                      xe[:, k * TT:(k + 1) * TT],
                                      start=(k == 0), stop=(k == NCH - 1))
                    wb_gate[oc % 2] = P.inc(i1, 'T')
                    v.wait_ge(P.sems['T'], P.counts['T'])
                    imu = v.tensor_mul(h[:, oc * TT:(oc + 1) * TT],
                                       h[:, oc * TT:(oc + 1) * TT], ps[:, 0:TT])
                    P.inc(imu, 'V')
                for oc in range(NCH):
                    buf = wb[oc % 2]
                    sy.wait_ge(P.sems['T'], wb_gate[oc % 2])
                    P.dma(sy, buf[:, 0:1024],
                          wts['dn_pan'][e8, oc][:, half * 1024:(half + 1) * 1024], 'L')
                    t.wait_ge(P.sems['L'], P.counts['L'])
                    t.wait_ge(P.sems['V'], P.counts['V'])
                    for k in range(8):
                        i2 = t.matmul(ps2[:, 0:TT], buf[:, k * 128:(k + 1) * 128],
                                      h[:, k * TT:(k + 1) * TT],
                                      start=(k == 0), stop=(k == 7))
                    wb_gate[oc % 2] = P.inc(i2, 'T')
                    v.wait_ge(P.sems['T'], P.counts['T'])
                    xt = x_tile(xs, oc, tt)
                    iacc = v.tensor_add(xt, xt, ps2[:, 0:TT])
                    P.inc(iacc, 'V')

    emit_postln(P, sc, xs)



def attn_slopes(n=16):
    import math
    def pow2(m):
        start = 2 ** (-2 ** (-(math.log2(m) - 3)))
        return [start * start ** i for i in range(m)]
    if math.log2(n).is_integer():
        return pow2(n)
    c = 2 ** math.floor(math.log2(n))
    return pow2(c) + attn_slopes(2 * c)[0::2][:n - c]


ATTN_SCALE = {'L1a': 1.0 / 8.0, 'L2a': 1.0 / 8.0, 'L3a': 1.0 / 32.0}


def attn_kcs(which, qt):
    if which == 'L1a':
        return list(range(8 * qt + 8))
    return list(range(32))


def l1a_mask_tiles():
    out = []
    for core in range(NCORES):
        ph = core % 2
        tiles = np.zeros((32, 128, 512), np.float32)
        for qt in range(NTT):
            qlo = qt * 1024 + 512 * ph
            for i, kc in enumerate(range(8 * qt, 8 * qt + 8)):
                k = kc * 128 + np.arange(128)[:, None]
                q = qlo + np.arange(512)[None, :]
                tiles[qt * 8 + i] = np.where(k <= q, 0.0, -1e30)
        out.append(np.ascontiguousarray(tiles))
    return out


L2A_NREL = sum(8 * qt + 8 for qt in range(NTT))


def l2a_rel_idx(qt, kc):
    return sum(8 * q + 8 for q in range(qt)) + kc


def l2a_rel_tiles():
    out = []
    for core in range(NCORES):
        ph = core % 2
        tiles = np.zeros((L2A_NREL, 128, 512), np.float32)
        for qt in range(NTT):
            qlo = qt * 1024 + 512 * ph
            for kc in range(8 * qt + 8):
                k = kc * 128 + np.arange(128)[:, None]
                q = qlo + np.arange(512)[None, :]
                tiles[l2a_rel_idx(qt, kc)] = np.minimum(k - q, 0).astype(np.float32)
        out.append(np.ascontiguousarray(tiles))
    return out


def prep_attn(P, inputs, which):
    w = {}
    if which == 'L1a':
        for nm, key in (('q', 'L1_attn_q_heads'), ('k', 'L1_attn_k_heads'),
                        ('v', 'L1_attn_v_heads')):
            w[nm + '_pan'] = P.dram_in(f'{which}_{nm}', pack_panels(
                np.ascontiguousarray(np.asarray(inputs[key]).T)))
        w['o_pan'] = P.dram_in(f'{which}_o', pack_panels(
            np.ascontiguousarray(np.asarray(inputs['L1_attn_output_proj']).T)))
        w['cmask'] = P.dram_in(f'{which}_cm', l1a_mask_tiles())
    elif which == 'L2a':
        qkv = np.asarray(inputs['L2_attn_W_qkv'])
        w['q_pan'] = P.dram_in(f'{which}_q', pack_panels(np.ascontiguousarray(qkv[:, 0:1024])))
        w['k_pan'] = P.dram_in(f'{which}_k', pack_panels(np.ascontiguousarray(qkv[:, 1024:2048])))
        w['v_pan'] = P.dram_in(f'{which}_v', pack_panels(np.ascontiguousarray(qkv[:, 2048:3072])))
        w['o_pan'] = P.dram_in(f'{which}_o', pack_panels(
            np.ascontiguousarray(np.asarray(inputs['L2_attn_W_proj']))))
        bq = np.asarray(inputs['L2_attn_b_qkv']).astype(np.float32)
        w['bqkv'] = P.dram_in(f'{which}_bqkv', np.ascontiguousarray(
            bq.reshape(3, 8, 128).transpose(2, 0, 1).reshape(128, 24)))
        w['bo'] = P.dram_in(f'{which}_bo', np.ascontiguousarray(
            np.asarray(inputs['L2_attn_b_proj']).reshape(8, 128).T.astype(np.float32)))
        w['rel'] = P.dram_in(f'{which}_rel', l2a_rel_tiles())
    elif which == 'L3a':
        qkv = np.asarray(inputs['L3_attn_w_qkv'])
        w['q_pan'] = P.dram_in(f'{which}_q', pack_panels(np.ascontiguousarray(qkv[:, 0:1024])))
        w['k_pan'] = P.dram_in(f'{which}_k', pack_panels(np.ascontiguousarray(qkv[:, 1024:2048])))
        w['v_pan'] = P.dram_in(f'{which}_v', pack_panels(np.ascontiguousarray(qkv[:, 2048:3072])))
        w['o_pan'] = P.dram_in(f'{which}_o', pack_panels(
            np.ascontiguousarray(np.asarray(inputs['L3_attn_w_out']))))
        w['bo'] = P.dram_in(f'{which}_bo', np.ascontiguousarray(
            np.asarray(inputs['L3_attn_b_out']).reshape(8, 128).T.astype(np.float32)))
        vk = np.asarray(inputs['L3_attn_void_k']).reshape(1024)
        w['voidk'] = P.dram_in(f'{which}_vk', np.ascontiguousarray(
            vk.reshape(8, 128).T.astype(np.float32)))
        vv = np.asarray(inputs['L3_attn_void_v']).reshape(16, 64)
        vrow = np.zeros((1, 1040), np.float32)
        for h in range(16):
            vrow[0, h * 65:h * 65 + 64] = vv[h]
            vrow[0, h * 65 + 64] = 1.0
        w['voidv'] = P.dram_in(f'{which}_vv', vrow)
        w['trace'] = P.dram_in(f'{which}_tr', np.ascontiguousarray(
            np.asarray(inputs['L3_attn_attention_trace']).reshape(1, 16).astype(np.float32)))
        w['tfac'] = P.dram_in(f'{which}_tf', np.ascontiguousarray(
            np.asarray(inputs['L3_attn_temperature_factor']).reshape(1, 16).astype(np.float32)))
    hh = np.zeros((128, 2), np.float32)
    hh[0:64, 0] = 1.0
    hh[64:128, 1] = 1.0
    w['headhalf_c'] = P.dram_in(f'{which}_hh', hh)
    w['ag_in'] = P.dram_tmp(f'{which}_agin', [16, 128, 2064], F32)
    w['ag_out'] = P.dram_tmp(f'{which}_agout', [32, 128, 2064], F32)
    w['agn_in'] = P.dram_tmp(f'{which}_agnin', [1, 16], F32)
    w['agn_out'] = P.dram_tmp(f'{which}_agnout', [8, 16], F32)
    return w


def alloc_attn(P, sc, which):
    sc['at_qo'] = P.sbuf('at_qo', [128, NCH * TOK], F32)
    sc['at_stg'] = P.sbuf('at_stg', [128, TT], F32)
    sc['at_sq'] = P.sbuf('at_sq', [128, TT], F32)
    sc['at_vstg'] = P.sbuf('at_vstg', [128, 65], F32)
    sc['at_ka'] = P.sbuf('at_ka', [128, 2064], F32)
    sc['at_kb'] = P.sbuf('at_kb', [128, 2064], F32)
    sc['at_ma'] = P.sbuf('at_ma', [128, TT], F32)
    sc['at_mb'] = P.sbuf('at_mb', [128, TT], F32)
    sc['at_pt'] = P.sbuf('at_pt', [128, 2 * TT], F32)
    sc['at_hh'] = P.sbuf('at_hh', [128, 2], F32)
    sc['at_row'] = P.sbuf('at_row', [1, 64], F32)
    sc['at_kn16'] = P.sbuf('at_kn16', [16, 8], F32)
    sc['at_bias'] = P.sbuf('at_bias', [128, 16], F32)
    sc['at_osb'] = P.sbuf('at_osb', [65, TT], F32)
    sc['at_wq'] = P.sbuf('at_wq', [128, 1024], F32)
    if which == 'L2a':
        sc['at_bq'] = P.sbuf('at_bq', [128, 24], F32)
    if which in ('L2a', 'L3a'):
        sc['at_bo'] = P.sbuf('at_bo', [128, 8], F32)
    if which == 'L3a':
        sc['at_vk'] = P.sbuf('at_vk', [128, 8], F32)
        sc['at_vv'] = P.sbuf('at_vv', [1, 1040], F32)
        sc['at_t1'] = P.sbuf('at_t1', [1, 16], F32)
        sc['at_t2'] = P.sbuf('at_t2', [1, 16], F32)
        sc['at_rcol'] = P.sbuf('at_rcol', [128, 16], F32)
    alloc_statrows(P, sc)


def emit_attn(P, sc, xs, wts, which):
    t, v, s, g, sy = P.t, P.v, P.s, P.g, P.sy
    qo = sc['at_qo']
    pool = sc['pspool']
    ps, ps2, ps_bc = sc['ps_mm'], sc['ps_mm2'], sc['ps_bc']
    kab = [sc['at_ka'], sc['at_kb']]
    mab = [sc['at_ma'], sc['at_mb']]
    scale = ATTN_SCALE[which]
    row = sc['at_row']
    slopes = attn_slopes(16)

    P.dma(sy, sc['at_hh'][:], wts['headhalf_c'], 'L')
    if which == 'L2a':
        P.dma(sy, sc['at_bq'][:], wts['bqkv'], 'L')
    if which in ('L2a', 'L3a'):
        P.dma(sy, sc['at_bo'][:], wts['bo'], 'L')
    if which == 'L3a':
        for key, ap in (('at_vk', wts['voidk']), ('at_vv', wts['voidv']),
                        ('at_t1', wts['trace']), ('at_t2', wts['tfac'])):
            P.dma(sy, sc[key][:], ap, 'L')
    for e in (t, v, s):
        P.barrier(e, ('T', 'V', 'A', 'L'))
    ivz = v.memset(row[0:1, 0:64], 0.0)
    P.inc(ivz, 'V')
    iv0b = v.memset(sc['at_vstg'][:, 64:65], 1.0)
    P.inc(iv0b, 'V')

    if which == 'L3a':
        ia = s.activation(row[0:1, 32:48], sc['at_t1'][0:1, :], AF.Abs)
        P.inc(ia, 'A')
        v.wait_ge(P.sems['A'], P.counts['A'])
        v.tensor_mul(row[0:1, 32:48], row[0:1, 32:48], sc['at_t2'][0:1, :])
        i2 = v.tensor_scalar(row[0:1, 32:48], row[0:1, 32:48], 1.0, 1.0,
                             AluOpType.add, AluOpType.max)
        P.inc(i2, 'V')
        irc = v.reciprocal(row[0:1, 48:64], row[0:1, 32:48])
        P.inc(irc, 'V')
        t.wait_ge(P.sems['V'], P.counts['V'])
        ib = t.matmul(ps_bc[:, 0:16], sc['onesrow'][:, :], row[0:1, 48:64],
                      start=True, stop=True)
        P.inc(ib, 'T')
        v.wait_ge(P.sems['T'], P.counts['T'])
        ic = v.tensor_copy(sc['at_rcol'][:, :], ps_bc[:, 0:16])
        P.inc(ic, 'V')
        t.wait_ge(P.sems['V'], P.counts['V'])

    def norms_update(src_sbuf, c, dstbase):
        s.wait_ge(P.sems['T'], P.counts['T'])
        isq = s.activation(sc['at_sq'][:, 0:TT], src_sbuf, AF.Square)
        P.inc(isq, 'A')
        for hi in range(2):
            t.wait_ge(P.sems['A'], P.counts['A'])
            t.wait_ge(P.sems['V'], P.counts['V'])
            ii = t.matmul(pool[0:1, 7 * TT:7 * TT + TT], sc['at_hh'][:, hi:hi + 1],
                          sc['at_sq'][:, 0:TT], start=True, stop=True)
            P.inc(ii, 'T')
            v.wait_ge(P.sems['T'], P.counts['T'])
            irm = v.reduce_max(row[0:1, 60:61], pool[0:1, 7 * TT:7 * TT + TT], axis=AX.X)
            P.inc(irm, 'V')
            v.wait_ge(P.sems['V'], P.counts['V'])
            dcol = dstbase + 2 * c + hi
            imx = v.tensor_scalar_max(row[0:1, dcol:dcol + 1], row[0:1, 60:61],
                                      row[0:1, dcol:dcol + 1])
            P.inc(imx, 'V')


    for kind in ('q', 'k', 'v'):
        pan = wts[kind + '_pan']
        for tt in range(NTT):
            for c in range(NCH):
                sy.wait_ge(P.sems['T'], P.counts['T'])
                P.dma(sy, sc['at_wq'][:, :], pan[c], 'L')
                t.wait_ge(P.sems['L'], P.counts['L'])
                t.wait_ge(P.sems['V'], P.counts['V'])
                t.wait_ge(P.sems['A'], P.counts['A'])
                for k in range(NCH):
                    i1 = t.matmul(ps[:, 0:TT], sc['at_wq'][:, k * 128:(k + 1) * 128],
                                  xs[:, k * TOK + tt * TT:k * TOK + (tt + 1) * TT],
                                  start=(k == 0), stop=(k == NCH - 1))
                P.inc(i1, 'T')
                v.wait_ge(P.sems['T'], P.counts['T'])
                if kind == 'q':
                    dst = qo[:, c * TOK + tt * TT: c * TOK + (tt + 1) * TT]
                    if which == 'L2a':
                        ie = v.tensor_scalar_add(dst, ps[:, 0:TT], sc['at_bq'][:, c:c + 1])
                        P.inc(ie, 'V')
                    elif which == 'L3a':
                        ie1 = v.tensor_scalar_mul(dst[0:64, :], ps[0:64, 0:TT],
                                                  sc['at_rcol'][0:64, 2 * c:2 * c + 1])
                        ie = v.tensor_scalar_mul(dst[64:128, :], ps[64:128, 0:TT],
                                                 sc['at_rcol'][64:128, 2 * c + 1:2 * c + 2])
                        P.inc(ie, 'V')
                    else:
                        ie = v.tensor_copy(dst, ps[:, 0:TT])
                        P.inc(ie, 'V')
                    norms_update(dst, c, 0)
                elif kind == 'k':
                    if which == 'L2a':
                        ie = v.tensor_scalar_add(sc['at_stg'][:, :], ps[:, 0:TT],
                                                 sc['at_bq'][:, 8 + c:9 + c])
                    else:
                        ie = v.tensor_copy(sc['at_stg'][:, :], ps[:, 0:TT])
                    P.inc(ie, 'V')
                    g.wait_ge(P.sems['V'], P.counts['V'])
                    for kk in range(4):
                        kc = tt * 4 + kk
                        P.dma(g, wts['ag_in'][kc][:, c * 128:(c + 1) * 128],
                              sc['at_stg'][:, kk * 128:(kk + 1) * 128], 'G')
                    norms_update(sc['at_stg'][:, :], c, 16)
                    v.wait_ge(P.sems['G'], P.counts['G'])
                else:
                    if which == 'L2a':
                        ie = v.tensor_scalar_add(sc['at_stg'][:, :], ps[:, 0:TT],
                                                 sc['at_bq'][:, 16 + c:17 + c])
                    else:
                        ie = v.tensor_copy(sc['at_stg'][:, :], ps[:, 0:TT])
                    P.inc(ie, 'V')
                    for kk in range(4):
                        kc = tt * 4 + kk
                        t.wait_ge(P.sems['V'], P.counts['V'])
                        itp = t.transpose(ps2[:, 0:128],
                                          sc['at_stg'][:, kk * 128:(kk + 1) * 128],
                                          sc['ident'][:, :])
                        P.inc(itp, 'T')
                        v.wait_ge(P.sems['T'], P.counts['T'])
                        for hi in range(2):
                            hhd = 2 * c + hi
                            v.wait_ge(P.sems['G'], P.counts['G'])
                            iev = v.tensor_copy(sc['at_vstg'][:, 0:64],
                                                ps2[:, hi * 64:hi * 64 + 64])
                            P.inc(iev, 'V')
                            g.wait_ge(P.sems['V'], P.counts['V'])
                            P.dma(g, wts['ag_in'][kc][:, 1024 + hhd * 65:1024 + hhd * 65 + 65],
                                  sc['at_vstg'][:, 0:65], 'G')
                    t.wait_ge(P.sems['G'], P.counts['G'])

    P.dump('at_qo1', qo[:, :], [128, NCH * TOK])
    if DBG.get('attn_stage') == 1:
        return
    g.wait_ge(P.sems['V'], P.counts['V'])
    P.dma(g, wts['agn_in'], row[0:1, 16:32], 'G')
    P.barrier(g)
    icc = g.collective_compute(
        "AllGather", AluOpType.bypass,
        replica_groups=[[0, 1], [2, 3], [4, 5], [6, 7]],
        ins=[wts['ag_in'].opt()], outs=[wts['ag_out'].opt()])
    P.inc(icc, 'C', 1)
    icc2 = g.collective_compute(
        "AllGather", AluOpType.bypass, replica_groups=[list(range(NCORES))],
        ins=[wts['agn_in'].opt()], outs=[wts['agn_out'].opt()])
    P.inc(icc2, 'C', 1)
    g.wait_ge(P.sems['C'], P.counts['C'])
    ign = g.dma_start(out=sc['at_kn16'][:, :], in_=wts['agn_out'].rearrange("r h -> h r"),
                      allow_slow_non_contiguous=True)
    P.inc(ign, 'G', 16)
    for e in (t, v, s, sy):
        P.wait(e, 'G')
    irk = v.tensor_reduce(sc['at_kn16'][:, 0:1], sc['at_kn16'][:, 0:8], axis=AX.X,
                          op=AluOpType.max)
    P.inc(irk, 'V')
    v.wait_ge(P.sems['V'], P.counts['V'])
    t.wait_ge(P.sems['V'], P.counts['V'])
    itk = t.transpose(ps_bc[0:1, 0:16], sc['at_kn16'][:, 0:1], sc['ident'][0:16, 0:16])
    P.inc(itk, 'T')
    v.wait_ge(P.sems['T'], P.counts['T'])
    ivk = v.tensor_copy(row[0:1, 32:48], ps_bc[0:1, 0:16])
    P.inc(ivk, 'V')
    s.wait_ge(P.sems['V'], P.counts['V'])
    is1 = s.activation(row[0:1, 0:16], row[0:1, 0:16], AF.Sqrt)
    is2 = s.activation(row[0:1, 32:48], row[0:1, 32:48], AF.Sqrt)
    P.inc(is2, 'A')
    v.wait_ge(P.sems['A'], P.counts['A'])
    im = v.tensor_mul(row[0:1, 0:16], row[0:1, 0:16], row[0:1, 32:48])
    P.inc(im, 'V')
    imm = v.tensor_scalar_mul(row[0:1, 0:16], row[0:1, 0:16], -scale)
    P.inc(imm, 'V')
    t.wait_ge(P.sems['V'], P.counts['V'])
    ibb = t.matmul(ps_bc[:, 0:16], sc['onesrow'][:, :], row[0:1, 0:16],
                   start=True, stop=True)
    P.inc(ibb, 'T')
    v.wait_ge(P.sems['T'], P.counts['T'])
    ibc = v.tensor_copy(sc['at_bias'][:, :], ps_bc[:, 0:16])
    P.inc(ibc, 'V')
    t.wait_ge(P.sems['V'], P.counts['V'])

    P.dump('at_bias1', sc['at_bias'][:, :], [128, 16])
    P.dump('at_ag0', wts['ag_out'][0], [128, 2064])
    P.dump('at_ag16', wts['ag_out'][16], [128, 2064])
    if DBG.get('attn_stage') == 2:
        return
    hgroups = [list(range(0, 6)), list(range(6, 12)), list(range(12, 16))]
    stb = [ps, ps2]
    ptsl = [sc['at_pt'][:, 0:TT], sc['at_pt'][:, TT:2 * TT]]
    kv_gate = [0, 0]
    m_gate = [0, 0]
    stb_gate = [0, 0]
    pt_gate = [0, 0]

    for qt in range(NTT):
        kcs = attn_kcs(which, qt)
        for gi, hg in enumerate(hgroups):
            obank = {h: pool[0:65, (2 + i) * TT:(2 + i) * TT + TT]
                     for i, h in enumerate(hg)}
            for ki, kc in enumerate(kcs):
                buf = kab[ki % 2]
                sy.wait_ge(P.sems['T'], kv_gate[ki % 2])
                blk_g = kc // 4
                agrow = (blk_g % 2) * 16 + (blk_g // 2) * 4 + kc % 4
                P.dma(sy, buf[:, :], wts['ag_out'][agrow], 'L')
                need_mask = (which == 'L1a' and kc >= 8 * qt)
                need_rel = (which == 'L2a' and kc < 8 * qt + 8)
                if need_mask or need_rel:
                    mbuf = mab[ki % 2]
                    sy.wait_ge(P.sems['V'], m_gate[ki % 2])
                    if need_mask:
                        midx = qt * 8 + (kc - 8 * qt)
                        P.dma(sy, mbuf[:, :], wts['cmask'][midx], 'L')
                    else:
                        P.dma(sy, mbuf[:, :], wts['rel'][l2a_rel_idx(qt, kc)], 'L')
                t.wait_ge(P.sems['L'], P.counts['L'])
                v.wait_ge(P.sems['L'], P.counts['L'])
                for hi, h in enumerate(hg):
                    cq = h // 2
                    base = (h % 2) * 64
                    sb = stb[hi % 2]
                    qrhs = qo[base:base + 64, cq * TOK + qt * TT:cq * TOK + (qt + 1) * TT]
                    t.wait_ge(P.sems['A'], stb_gate[hi % 2])
                    if need_mask or need_rel:
                        t.wait_ge(P.sems['V'], P.counts['V'])
                    iqk = t.matmul(sb[:, 0:TT],
                                   buf[base:base + 64, cq * 128:(cq + 1) * 128],
                                   qrhs, start=True, stop=True)
                    P.inc(iqk, 'T')
                    if need_mask:
                        v.wait_ge(P.sems['T'], P.counts['T'])
                        imsk = v.tensor_add(sb[:, 0:TT], sb[:, 0:TT], mbuf[:, :])
                        P.inc(imsk, 'V')
                        s.wait_ge(P.sems['V'], P.counts['V'])
                    elif need_rel:
                        v.wait_ge(P.sems['T'], P.counts['T'])
                        imsk = v.scalar_tensor_tensor(sb[:, 0:TT], mbuf[:, :],
                                                      slopes[h] / scale, sb[:, 0:TT],
                                                      AluOpType.mult, AluOpType.add)
                        P.inc(imsk, 'V')
                        s.wait_ge(P.sems['V'], P.counts['V'])
                    s.wait_ge(P.sems['T'], P.counts['T'])
                    s.wait_ge(P.sems['T'], pt_gate[hi % 2])
                    iex = s.activation(ptsl[hi % 2], sb[:, 0:TT], AF.Exp,
                                       bias=sc['at_bias'][:, h:h + 1], scale=scale)
                    stb_gate[hi % 2] = P.inc(iex, 'A')
                    t.wait_ge(P.sems['A'], P.counts['A'])
                    ipv = t.matmul(obank[h], buf[:, 1024 + h * 65:1024 + h * 65 + 65],
                                   ptsl[hi % 2], start=(ki == 0),
                                   stop=(ki == len(kcs) - 1 and which != 'L3a'))
                    pt_gate[hi % 2] = P.inc(ipv, 'T')
                if need_mask or need_rel:
                    m_gate[ki % 2] = P.counts['V']
                kv_gate[ki % 2] = P.counts['T']
            if which == 'L3a':
                for hi, h in enumerate(hg):
                    cq = h // 2
                    base = (h % 2) * 64
                    sb = stb[hi % 2]
                    qrhs = qo[base:base + 64, cq * TOK + qt * TT:cq * TOK + (qt + 1) * TT]
                    t.wait_ge(P.sems['A'], stb_gate[hi % 2])
                    iqk = t.matmul(sb[0:1, 0:TT], sc['at_vk'][base:base + 64, cq:cq + 1],
                                   qrhs, start=True, stop=True)
                    P.inc(iqk, 'T')
                    s.wait_ge(P.sems['T'], P.counts['T'])
                    s.wait_ge(P.sems['T'], pt_gate[hi % 2])
                    iex = s.activation(ptsl[hi % 2][0:1, :], sb[0:1, 0:TT], AF.Exp,
                                       bias=sc['at_bias'][0:1, h:h + 1], scale=scale)
                    stb_gate[hi % 2] = P.inc(iex, 'A')
                    t.wait_ge(P.sems['A'], P.counts['A'])
                    ipv = t.matmul(obank[h], sc['at_vv'][0:1, h * 65:h * 65 + 65],
                                   ptsl[hi % 2][0:1, :], start=False, stop=True)
                    pt_gate[hi % 2] = P.inc(ipv, 'T')
            for hi, h in enumerate(hg):
                cq = h // 2
                base = (h % 2) * 64
                s.wait_ge(P.sems['T'], P.counts['T'])
                s.wait_ge(P.sems['V'], P.counts['V'])
                iln = s.activation(sc['at_osb'][0:1, 0:TT], obank[h][64:65, 0:TT], AF.Ln)
                P.inc(iln, 'A')
                iexp = s.activation(sc['at_osb'][0:1, 0:TT], sc['at_osb'][0:1, 0:TT],
                                    AF.Exp, scale=-1.0)
                P.inc(iexp, 'A')
                t.wait_ge(P.sems['A'], P.counts['A'])
                ibc2 = t.matmul(ps[:, 0:TT],
                                sc['onesrow'][:, :], sc['at_osb'][0:1, 0:TT],
                                start=True, stop=True)
                P.inc(ibc2, 'T')
                v.wait_ge(P.sems['T'], P.counts['T'])
                icp = v.tensor_copy(sc['at_osb'][0:64, 0:TT], obank[h][0:64, 0:TT])
                P.inc(icp, 'V')
                imul = v.tensor_mul(qo[base:base + 64, cq * TOK + qt * TT:
                                       cq * TOK + (qt + 1) * TT],
                                    sc['at_osb'][0:64, 0:TT], ps[0:64, 0:TT])
                P.inc(imul, 'V')
                t.wait_ge(P.sems['V'], P.counts['V'])
                stb_gate = [max(stb_gate[0], P.counts['A']), max(stb_gate[1], P.counts['A'])]

    P.dump('at_out1', qo[:, :], [128, NCH * TOK])
    if DBG.get('attn_stage') == 3:
        return
    for e in (t, v, s):
        P.barrier(e, ('T', 'V', 'A'))
    for tt in range(NTT):
        for c in range(NCH):
            sy.wait_ge(P.sems['T'], P.counts['T'])
            P.dma(sy, sc['at_wq'][:, :], wts['o_pan'][c], 'L')
            t.wait_ge(P.sems['L'], P.counts['L'])
            t.wait_ge(P.sems['V'], P.counts['V'])
            for k in range(NCH):
                i1 = t.matmul(ps[:, 0:TT], sc['at_wq'][:, k * 128:(k + 1) * 128],
                              qo[:, k * TOK + tt * TT:k * TOK + (tt + 1) * TT],
                              start=(k == 0), stop=(k == NCH - 1))
            P.inc(i1, 'T')
            v.wait_ge(P.sems['T'], P.counts['T'])
            xt = x_tile(xs, c, tt)
            if which in ('L2a', 'L3a'):
                iv2 = v.scalar_tensor_tensor(xt, ps[:, 0:TT], sc['at_bo'][:, c:c + 1],
                                             xt, AluOpType.add, AluOpType.add)
            else:
                iv2 = v.tensor_add(xt, xt, ps[:, 0:TT])
            P.inc(iv2, 'V')

    emit_postln(P, sc, xs)




RES_SCALE = float(1.0 / np.sqrt(2.0))
EPS_FP32 = float(np.finfo(np.float32).eps)


def prep_L0a(P, inputs):
    ipw = np.asarray(inputs['L0_attn_in_proj_w'])
    opw = np.asarray(inputs['L0_attn_out_proj_w'])
    fcw = np.asarray(inputs['L0_attn_fc_w'])
    pjw = np.asarray(inputs['L0_attn_proj_w'])
    mask = np.ones((128, 512), np.float32)
    mask[:, 0::64] = 0.0
    hh2 = np.zeros((2, 128), np.float32)
    hh2[0, 0:64] = 1.0
    hh2[1, 64:128] = 1.0
    hhc = np.zeros((128, 2), np.float32)
    hhc[0:64, 0] = 1.0
    hhc[64:128, 1] = 1.0
    ceps = np.zeros((128, 3), np.float32)
    ceps[:, 0] = 1e-6
    ceps[:, 1] = EPS_FP32
    ceps[:, 2] = 2.0
    wsel = []
    for c in range(NCORES):
        col = np.zeros((128, 2), np.float32)
        col[:, c % 2] = 1.0
        wsel.append(np.ascontiguousarray(col))
    return {
        'in_pan': P.dram_in('L0a_in', pack_panels(np.ascontiguousarray(ipw.T))),
        'out_pan': P.dram_in('L0a_out', pack_panels(np.ascontiguousarray(opw.T))),
        'fc_pan': P.dram_in('L0a_fc', pack_panels(np.ascontiguousarray(fcw.T))),
        'pj_pan': P.dram_in('L0a_pj', pack_panels(np.ascontiguousarray(pjw.T))),
        'n1': P.dram_in('L0a_n1', np.ascontiguousarray(
            np.asarray(inputs['L0_attn_norm1_w']).reshape(8, 128).T.astype(np.float32))),
        'n2': P.dram_in('L0a_n2', np.ascontiguousarray(
            np.asarray(inputs['L0_attn_norm2_w']).reshape(8, 128).T.astype(np.float32))),
        'scanmask': P.dram_in('L0a_sm', np.ascontiguousarray(mask)),
        'hh2': P.dram_in('L0a_hh2', np.ascontiguousarray(hh2)),
        'hhc': P.dram_in('L0a_hhc', np.ascontiguousarray(hhc)),
        'ceps': P.dram_in('L0a_ceps', np.ascontiguousarray(ceps)),
        'wsel': P.dram_in('L0a_wsel', wsel),
        'spill': P.dram_tmp('L0a_spill', [NTT, 5, 128, 8 * TT], F32),
        'xsave': P.dram_tmp('L0a_xsave', [128, NCH * TOK], F32),
        'ag_in': P.dram_tmp('L0a_agin', [2, 128, 256], F32),
        'ag_out': P.dram_tmp('L0a_agout', [4, 128, 256], F32),
    }


def alloc_L0a(P, sc):
    for nm in ('b_q', 'b_v', 'b_g', 'b_pf', 'b_u'):
        sc[nm] = P.sbuf(nm, [128, 8 * TT], F32)
    sc['l0_h1'] = P.sbuf('l0_h1', [128, 8 * TT], F32)
    sc['l0_wa'] = P.sbuf('l0_wa', [128, 4096], F32)
    sc['l0_s1'] = P.sbuf('l0_s1', [128, TT], F32)
    sc['l0_s2'] = P.sbuf('l0_s2', [128, TT], F32)
    sc['l0_rw'] = P.sbuf('l0_rw', [2, TT], F32)
    sc['l0_n1'] = P.sbuf('l0_n1', [128, 8], F32)
    sc['l0_n2'] = P.sbuf('l0_n2', [128, 8], F32)
    sc['l0_sm'] = P.sbuf('l0_sm', [128, TT], F32)
    sc['l0_hh2'] = P.sbuf('l0_hh2', [2, 128], F32)
    sc['l0_wsel'] = P.sbuf('l0_wsel', [128, 2], F32)
    sc['l0_pa'] = P.sbuf('l0_pa', [128, 1024], F32)
    sc['l0_sA'] = P.sbuf('l0_sA', [128, 256], F32)
    sc['l0_sB'] = P.sbuf('l0_sB', [128, 256], F32)
    sc['l0_ssel'] = P.sbuf('l0_ssel', [128, 256], F32)
    sc['l0_st'] = P.sbuf('l0_st', [128, 16], F32)
    sc['l0_ps'] = P.sbuf('l0_ps', [128, 8], F32)
    sc['l0_as'] = P.sbuf('l0_as', [128, 8], F32)
    sc['l0_hh'] = P.sbuf('l0_hh', [128, 2], F32)
    sc['l0_ceps'] = P.sbuf('l0_ceps', [128, 3], F32)
    sc['l0_rms'] = P.sbuf('l0_rms', [1, TT], F32)
    bu = sc['b_u']
    sc['st_mean'] = bu[0:1, 0:TOK]
    sc['st_var'] = bu[0:1, TOK:2 * TOK]
    sc['st_tmp'] = sc['b_pf'][0:1, 0:TOK]


def emit_L0a(P, sc, xs, wts):
    t, v, s, g, sy = P.t, P.v, P.s, P.g, P.sy
    ps, ps2, ps_bc = sc['ps_mm'], sc['ps_mm2'], sc['ps_bc']
    ps_s = sc['ps_stat']
    bq, bv, bg = sc['b_q'], sc['b_v'], sc['b_g']
    bpf, bu = sc['b_pf'], sc['b_u']
    bac = bu
    h1, wa = sc['l0_h1'], sc['l0_wa']
    s1, s2, rw = sc['l0_s1'], sc['l0_s2'], sc['l0_rw']

    for key, ap in (('l0_n1', wts['n1']), ('l0_n2', wts['n2']),
                    ('l0_sm', wts['scanmask']), ('l0_hh2', wts['hh2']),
                    ('l0_hh', wts['hhc']), ('l0_ceps', wts['ceps']),
                    ('l0_wsel', wts['wsel'])):
        P.dma(sy, sc[key][:], ap, 'L')
    for e in (t, v, s, g):
        P.barrier(e, ('T', 'V', 'A', 'L'))
    P.dma(g, wts['xsave'], xs[:, :], 'G')

    for tt in range(NTT):
        P.fence()
        tiles = [x_tile(xs, c, tt) for c in range(NCH)]
        sumsq_over_chunks(P, sc, tiles, TT, ps_s)
        s.wait_ge(P.sems['T'], P.counts['T'])
        i1 = s.activation(sc['l0_rms'][0:1, 0:TT], ps_s[0:1, 0:TT], AF.Ln,
                          bias=sc['l0_ceps'][0:1, 0:1], scale=1.0 / D)
        i2 = s.activation(sc['l0_rms'][0:1, 0:TT], sc['l0_rms'][0:1, 0:TT], AF.Exp,
                          scale=-0.5)
        P.inc(i2, 'A')
        t.wait_ge(P.sems['A'], P.counts['A'])
        bcast_row(P, sc, sc['l0_rms'][0:1, 0:TT], ps_bc[:, 0:TT])
        v.wait_ge(P.sems['T'], P.counts['T'])
        for c in range(NCH):
            ih = v.scalar_tensor_tensor(h1[:, c * TT:(c + 1) * TT], tiles[c],
                                        sc['l0_n1'][:, c:c + 1], ps_bc[:, 0:TT],
                                        AluOpType.mult, AluOpType.mult)
            P.inc(ih, 'V')

        for oc in range(40):
            sy.wait_ge(P.sems['T'], P.counts['T'])
            P.dma(sy, wa[:, 0:1024], wts['in_pan'][oc], 'L')
            t.wait_ge(P.sems['L'], P.counts['L'])
            t.wait_ge(P.sems['V'], P.counts['V'])
            t.wait_ge(P.sems['A'], P.counts['A'])
            for k in range(NCH):
                i1 = t.matmul(ps[:, 0:TT], wa[:, k * 128:(k + 1) * 128],
                              h1[:, k * TT:(k + 1) * TT],
                              start=(k == 0), stop=(k == NCH - 1))
            P.inc(i1, 'T')
            c = oc % 8
            dst = [bq, bu, bv, bpf, bg][oc // 8][:, c * TT:(c + 1) * TT]
            if oc < 8:
                v.wait_ge(P.sems['T'], P.counts['T'])
                ie = v.tensor_copy(dst, ps[:, 0:TT])
                P.inc(ie, 'V')
            elif oc < 16:
                s.wait_ge(P.sems['T'], P.counts['T'])
                ie = s.activation(dst, ps[:, 0:TT], AF.Tanh)
                P.inc(ie, 'A')
            elif oc < 24:
                v.wait_ge(P.sems['T'], P.counts['T'])
                ie = v.tensor_copy(dst, ps[:, 0:TT])
                P.inc(ie, 'V')
            elif oc < 32:
                s.wait_ge(P.sems['T'], P.counts['T'])
                ie = s.activation(dst, ps[:, 0:TT], AF.Sigmoid,
                                  bias=sc['l0_ceps'][:, 2:3])
                P.inc(ie, 'A')
                v.wait_ge(P.sems['A'], P.counts['A'])
                ic = v.tensor_scalar(dst, dst, 0.6, 0.9995,
                                     AluOpType.max, AluOpType.min)
                P.inc(ic, 'V')
            else:
                s.wait_ge(P.sems['T'], P.counts['T'])
                ie = s.activation(dst, ps[:, 0:TT], AF.Sigmoid)
                P.inc(ie, 'A')

        P.fence()
        for c in range(NCH):
            qd = bq[:, c * TT:(c + 1) * TT]
            s.wait_ge(P.sems['V'], P.counts['V'])
            s.wait_ge(P.sems['A'], P.counts['A'])
            isq = s.activation(sc['sq'][:, 0:TT], qd, AF.Square)
            P.inc(isq, 'A')
            t.wait_ge(P.sems['A'], P.counts['A'])
            t.wait_ge(P.sems['V'], P.counts['V'])
            im = t.matmul(ps2[0:2, 0:TT], sc['l0_hh'][:, :], sc['sq'][:, 0:TT],
                          start=True, stop=True)
            sc.setdefault('sq_slot_t', [0, 0])[0] = P.inc(im, 'T')
            s.wait_ge(P.sems['T'], P.counts['T'])
            ir1 = s.activation(rw[0:2, 0:TT], ps2[0:2, 0:TT], AF.Ln,
                               bias=sc['l0_ceps'][0:2, 1:2], scale=1.0 / 64.0)
            ir2 = s.activation(rw[0:2, 0:TT], rw[0:2, 0:TT], AF.Exp, scale=-0.5)
            P.inc(ir2, 'A')
            t.wait_ge(P.sems['A'], P.counts['A'])
            ib = t.matmul(ps2[:, 0:TT], sc['l0_hh2'][:, :], rw[0:2, 0:TT],
                          start=True, stop=True)
            P.inc(ib, 'T')
            v.wait_ge(P.sems['T'], P.counts['T'])
            iq = v.tensor_mul(qd, qd, ps2[:, 0:TT])
            P.inc(iq, 'V')
        P.fence()
        v.wait_ge(P.sems['A'], P.counts['A'])
        for c in range(NCH):
            iu = v.tensor_mul(bu[:, c * TT:(c + 1) * TT], bu[:, c * TT:(c + 1) * TT],
                              bv[:, c * TT:(c + 1) * TT])
            P.inc(iu, 'V')
        P.fence()
        for c in range(NCH):
            ad = bpf[:, c * TT:(c + 1) * TT]
            i1 = v.tensor_mul(s1[:, :], ad, sc['l0_sm'][:, :])
            i2 = v.tensor_sub(s2[:, :], ad, s1[:, :])
            P.inc(i2, 'V')
            isc = v.tensor_tensor_scan(ad, s1[:, :], s2[:, :], 1.0,
                                       AluOpType.mult, AluOpType.add)
            P.inc(isc, 'V')
            v.wait_ge(P.sems['V'], P.counts['V'])
            im1 = v.tensor_scalar_max(s1[:, :], ad, 1e-6)
            P.inc(im1, 'V')
            s.wait_ge(P.sems['V'], P.counts['V'])
            il = s.activation(s1[:, :], s1[:, :], AF.Ln)
            ix = s.activation(s1[:, :], s1[:, :], AF.Exp, scale=-1.0)
            P.inc(ix, 'A')
            v.wait_ge(P.sems['A'], P.counts['A'])
            iup = v.tensor_mul(s2[:, :], bu[:, c * TT:(c + 1) * TT], s1[:, :])
            P.inc(iup, 'V')
            isc2 = v.tensor_tensor_scan(bac[:, c * TT:(c + 1) * TT], sc['l0_sm'][:, :],
                                        s2[:, :], 0.0, AluOpType.mult, AluOpType.add)
            P.inc(isc2, 'V')
            v.wait_ge(P.sems['V'], P.counts['V'])
            v.wait_ge(P.sems['V'], P.counts['V'])
            ips = v.tensor_copy(sc['l0_ps'][:, 0:8],
                                ad.rearrange("p (n l) -> p n l", l=64)[:, :, 63:64].squeeze(2))
            ias = v.tensor_copy(sc['l0_as'][:, 0:8],
                                bac[:, c * TT:(c + 1) * TT]
                                .rearrange("p (n l) -> p n l", l=64)[:, :, 63:64].squeeze(2))
            P.inc(ias, 'V')
            g.wait_ge(P.sems['V'], P.counts['V'])
            P.dma(g, wts['ag_in'][0][:, c * 32 + tt * 8:c * 32 + tt * 8 + 8],
                  sc['l0_ps'][:, 0:8], 'G')
            P.dma(g, wts['ag_in'][1][:, c * 32 + tt * 8:c * 32 + tt * 8 + 8],
                  sc['l0_as'][:, 0:8], 'G')
            v.wait_ge(P.sems['G'], P.counts['G'])
        P.fence()
        g.wait_ge(P.sems['V'], P.counts['V'])
        g.wait_ge(P.sems['A'], P.counts['A'])
        for si, b in enumerate((bq, bv, bg, bpf, bac)):
            P.dma(g, wts['spill'][tt, si], b[:, :], 'G')
        v.wait_ge(P.sems['G'], P.counts['G'])
        s.wait_ge(P.sems['G'], P.counts['G'])

    for _si in range(5):
        P.dump(f'l0_spill0_{_si}', wts['spill'][0, _si], [128, 8 * TT])
    P.barrier(g)
    icc = g.collective_compute(
        "AllGather", AluOpType.bypass,
        replica_groups=[[0, 1], [2, 3], [4, 5], [6, 7]],
        ins=[wts['ag_in'].opt()], outs=[wts['ag_out'].opt()])
    P.inc(icc, 'C', 1)
    g.wait_ge(P.sems['C'], P.counts['C'])
    for r in range(4):
        P.dma(g, sc['l0_pa'][:, r * 256:(r + 1) * 256], wts['ag_out'][r], 'L')
    for e in (t, v, s, sy):
        P.wait(e, 'G')
        P.wait(e, 'L')
    st = sc['l0_st']
    iz = v.memset(st[:, 0:8], 0.0)
    P.inc(iz, 'V')
    pa = sc['l0_pa']
    for gc in range(64):
        pp = (gc // 8) % 2
        lc = (gc // 16) * 8 + gc % 8
        pview = pa[:, pp * 512: pp * 512 + 256].rearrange(
            "p (c l) -> p c l", l=32)[:, :, lc:lc + 1].squeeze(2)
        aview = pa[:, pp * 512 + 256: pp * 512 + 512].rearrange(
            "p (c l) -> p c l", l=32)[:, :, lc:lc + 1].squeeze(2)
        dstbuf = sc['l0_sA'] if pp == 0 else sc['l0_sB']
        ic = v.tensor_copy(dstbuf[:, lc * 8:(lc + 1) * 8], st[:, 0:8])
        P.inc(ic, 'V')
        i1 = v.tensor_add(st[:, 8:16], st[:, 0:8], aview)
        P.inc(i1, 'V')
        i2 = v.tensor_mul(st[:, 0:8], pview, st[:, 8:16])
        P.inc(i2, 'V')
    iw1 = v.tensor_scalar_mul(sc['l0_ssel'][:, :], sc['l0_sA'][:, :],
                              sc['l0_wsel'][:, 0:1])
    P.inc(iw1, 'V')
    iw2 = v.scalar_tensor_tensor(sc['l0_ssel'][:, :], sc['l0_sB'][:, :],
                                 sc['l0_wsel'][:, 1:2], sc['l0_ssel'][:, :],
                                 AluOpType.mult, AluOpType.add)
    P.inc(iw2, 'V')

    P.dump('l0_ssel', sc['l0_ssel'][:, :], [128, 256])
    for e in (t, v, s):
        P.barrier(e, ('T', 'V', 'A', 'G', 'L'))
    for tt in range(NTT):
        sy.wait_ge(P.sems['T'], P.counts['T'])
        sy.wait_ge(P.sems['V'], P.counts['V'])
        for si, b in enumerate((bq, bv, bg, bpf, bac)):
            P.dma(sy, b[:, :], wts['spill'][tt, si], 'L')
        for e in (t, v, s):
            P.barrier(e, ('L',))
        P.fence()
        for c in range(NCH):
            for ch in range(8):
                lc = tt * 8 + ch
                scol = sc['l0_ssel'][:, lc * 8 + c:lc * 8 + c + 1]
                sl = slice(c * TT + ch * 64, c * TT + (ch + 1) * 64)
                imem = v.scalar_tensor_tensor(bac[:, sl], bac[:, sl], scol,
                                              bpf[:, sl], AluOpType.add,
                                              AluOpType.mult)
                P.inc(imem, 'V')
        if tt == 0:
            P.dump('l0_mem0', bac[:, :], [128, 8 * TT])
        for c in range(NCH):
            sl = slice(c * TT, (c + 1) * TT)
            v.tensor_mul(bq[:, sl], bq[:, sl], bac[:, sl])
            v.tensor_mul(bq[:, sl], bq[:, sl], bg[:, sl])
            i3 = v.tensor_scalar(bg[:, sl], bg[:, sl], -1.0, 1.0,
                                 AluOpType.mult, AluOpType.add)
            P.inc(i3, 'V')
            v.tensor_mul(bv[:, sl], bv[:, sl], bg[:, sl])
            iy = v.tensor_add(bq[:, sl], bq[:, sl], bv[:, sl])
            P.inc(iy, 'V')
        if tt == 0:
            P.dump('l0_y0', bq[:, :], [128, 8 * TT])
        P.fence()
        for oc in range(NCH):
            sy.wait_ge(P.sems['T'], P.counts['T'])
            P.dma(sy, wa[:, 0:1024], wts['out_pan'][oc], 'L')
            t.wait_ge(P.sems['L'], P.counts['L'])
            t.wait_ge(P.sems['V'], P.counts['V'])
            for k in range(NCH):
                i1 = t.matmul(ps[:, 0:TT], wa[:, k * 128:(k + 1) * 128],
                              bq[:, k * TT:(k + 1) * TT],
                              start=(k == 0), stop=(k == NCH - 1))
            P.inc(i1, 'T')
            v.wait_ge(P.sems['T'], P.counts['T'])
            xt = x_tile(xs, oc, tt)
            iv2 = v.scalar_tensor_tensor(xt, ps[:, 0:TT], RES_SCALE, xt,
                                         AluOpType.mult, AluOpType.add)
            P.inc(iv2, 'V')

    P.dump('l0_premlp', xs[:, :], [128, NCH * TOK])
    for e in (t, v, s):
        P.barrier(e, ('T', 'V', 'A'))
    for tt in range(NTT):
        tiles = [x_tile(xs, c, tt) for c in range(NCH)]
        sumsq_over_chunks(P, sc, tiles, TT, ps_s)
        s.wait_ge(P.sems['T'], P.counts['T'])
        i1 = s.activation(sc['l0_rms'][0:1, 0:TT], ps_s[0:1, 0:TT], AF.Ln,
                          bias=sc['l0_ceps'][0:1, 0:1], scale=1.0 / D)
        i2 = s.activation(sc['l0_rms'][0:1, 0:TT], sc['l0_rms'][0:1, 0:TT], AF.Exp,
                          scale=-0.5)
        P.inc(i2, 'A')
        t.wait_ge(P.sems['A'], P.counts['A'])
        bcast_row(P, sc, sc['l0_rms'][0:1, 0:TT], ps_bc[:, 0:TT])
        v.wait_ge(P.sems['T'], P.counts['T'])
        v.wait_ge(P.sems['A'], P.counts['A'])
        for c in range(NCH):
            ih = v.scalar_tensor_tensor(h1[:, c * TT:(c + 1) * TT], tiles[c],
                                        sc['l0_n2'][:, c:c + 1], ps_bc[:, 0:TT],
                                        AluOpType.mult, AluOpType.mult)
            P.inc(ih, 'V')
        P.fence()
        for half in range(2):
            for oc in range(16):
                ocg = half * 16 + oc
                sy.wait_ge(P.sems['T'], P.counts['T'])
                P.dma(sy, wa[:, 0:1024], wts['fc_pan'][ocg], 'L')
                t.wait_ge(P.sems['L'], P.counts['L'])
                t.wait_ge(P.sems['A'], P.counts['A'])
                t.wait_ge(P.sems['V'], P.counts['V'])
                for k in range(NCH):
                    i1 = t.matmul(ps[:, 0:TT], wa[:, k * 128:(k + 1) * 128],
                                  h1[:, k * TT:(k + 1) * TT],
                                  start=(k == 0), stop=(k == NCH - 1))
                P.inc(i1, 'T')
                s.wait_ge(P.sems['T'], P.counts['T'])
                hdst = (bq if oc < 8 else bv)[:, (oc % 8) * TT:(oc % 8 + 1) * TT]
                isl = s.activation(hdst, ps[:, 0:TT], AF.Silu)
                P.inc(isl, 'A')
            for oc in range(NCH):
                sy.wait_ge(P.sems['T'], P.counts['T'])
                P.dma(sy, wa[:, 0:2048],
                      wts['pj_pan'][oc][:, half * 2048:(half + 1) * 2048], 'L')
                t.wait_ge(P.sems['L'], P.counts['L'])
                t.wait_ge(P.sems['A'], P.counts['A'])
                t.wait_ge(P.sems['V'], P.counts['V'])
                for k in range(16):
                    hsrc = (bq if k < 8 else bv)[:, (k % 8) * TT:(k % 8 + 1) * TT]
                    i2 = t.matmul(ps2[:, 0:TT], wa[:, k * 128:(k + 1) * 128],
                                  hsrc, start=(k == 0), stop=(k == 15))
                P.inc(i2, 'T')
                v.wait_ge(P.sems['T'], P.counts['T'])
                xt = x_tile(xs, oc, tt)
                iv2 = v.scalar_tensor_tensor(xt, ps2[:, 0:TT], RES_SCALE, xt,
                                             AluOpType.mult, AluOpType.add)
                P.inc(iv2, 'V')

    P.dump('l0_postmlp', xs[:, :], [128, NCH * TOK])
    for e in (sy,):
        P.barrier(e, ('T', 'V', 'A', 'G'))
    for half in range(4):
        P.dma(sy, wa[:, 0:4096], wts['xsave'][:, half * 4096:(half + 1) * 4096], 'L')
        v.wait_ge(P.sems['L'], P.counts['L'])
        iv2 = v.tensor_add(xs[:, half * 4096:(half + 1) * 4096],
                           xs[:, half * 4096:(half + 1) * 4096], wa[:, 0:4096])
        P.inc(iv2, 'V')
        sy.wait_ge(P.sems['V'], P.counts['V'])
    emit_postln(P, sc, xs)


BLOCKS = ['L0a', 'L0f', 'L1a', 'L1f', 'L2a', 'L2f', 'L3a', 'L3f']


def build_and_run(blocks, inputs, x_imgs):
    P = Prog('+'.join(blocks))
    Xin = P.dram_in('x_in', x_imgs)
    Xout = P.dram_out('x_out', [128, NCH * TOK], F32)
    wts = {}
    for b in blocks:
        wts[b] = PREP[b](P, inputs)

    with P.es:
        xs = P.sbuf('xs', [128, NCH * TOK], F32)
        sc = alloc_common(P)
        for b in blocks:
            ALLOC[b](P, sc)
        P.open_block()

        P.dma(P.sy, xs[:], Xin, 'L')
        consts = {
            'iota8': P.dram_in('c_iota8', np.tile(np.arange(8, dtype=np.float32), (128, 1))),
            'iota8b': P.dram_in('c_iota8b', np.tile(np.arange(8, dtype=np.float32) + np.float32(100.0), (128, 1))),
            'ustrict': P.dram_in('c_ustrict', np.triu(np.ones((128, 128), np.float32), 1)),
            'ident': P.dram_in('c_ident', np.eye(128, dtype=np.float32)),
        }
        init_common(P, sc, consts)
        for e in (P.t, P.v, P.s, P.g):
            P.wait(e, 'L')
            P.wait(e, 'V')

        for b in blocks:
            EMIT[b](P, sc, xs, wts[b])

        P.barrier(P.g)
        P.dma(P.g, Xout, xs[:], 'G')
        P.wait(P.g, 'G')

    res = run_bass_kernel_spmd(P.nc, P.in_maps(), core_ids=list(range(NCORES)))
    if 'on' in DBG:
        DBG['res'] = res.results
    return [np.asarray(r['x_out']) for r in res.results]


PREP = {'L0f': prep_L0f, 'L2f': prep_L2f, 'L1f': prep_L1f, 'L3f': prep_L3f,
        'L1a': lambda P, i: prep_attn(P, i, 'L1a'), 'L2a': lambda P, i: prep_attn(P, i, 'L2a'),
        'L3a': lambda P, i: prep_attn(P, i, 'L3a'), 'L0a': prep_L0a}
ALLOC = {'L0f': alloc_L0f, 'L2f': alloc_L2f, 'L1f': alloc_L1f, 'L3f': alloc_L3f,
         'L1a': lambda P, sc: alloc_attn(P, sc, 'L1a'), 'L2a': lambda P, sc: alloc_attn(P, sc, 'L2a'),
         'L3a': lambda P, sc: alloc_attn(P, sc, 'L3a'), 'L0a': alloc_L0a}
EMIT = {'L0f': emit_L0f, 'L2f': emit_L2f, 'L1f': emit_L1f, 'L3f': emit_L3f,
        'L1a': lambda P, sc, xs, w: emit_attn(P, sc, xs, w, 'L1a'),
        'L2a': lambda P, sc, xs, w: emit_attn(P, sc, xs, w, 'L2a'),
        'L3a': lambda P, sc, xs, w: emit_attn(P, sc, xs, w, 'L3a'), 'L0a': emit_L0a}


def kernel(**inputs):
    x = np.asarray(inputs['L0_attn_x'], dtype=np.float32)
    xf = x.reshape(16384, D)
    imgs = [to_fm(xf[core_token_idx(c)]) for c in range(NCORES)]
    for b in BLOCKS:
        imgs = build_and_run([b], inputs, imgs)
    out = np.zeros((16384, D), np.float32)
    for c in range(NCORES):
        out[core_token_idx(c)] = from_fm(imgs[c])
    return out.reshape(4, 4096, D)
